# version 06
# baseline (speedup 1.0000x reference)
import math
import numpy as np
import ml_dtypes
import concourse.bass as bass
import concourse.mybir as mybir
from concourse.bass_utils import run_bass_kernel_spmd

F32 = mybir.dt.float32
BF16 = mybir.dt.bfloat16
I32 = mybir.dt.int32
AF = mybir.ActivationFunctionType
ALU = mybir.AluOpType
AX = mybir.AxisListType.X
SP = mybir.EngineType.SP
POOL = mybir.EngineType.Pool

B, T, D = 2, 8192, 1024
NH, HD, RD, QDIM = 16, 64, 32, 96
KVR, QR, DFF = 256, 512, 4096
CH = 1024
TOWN = 2 * CH
SCALE = 1.0 / math.sqrt(96.0)
ESHIFT = -25.0
MASKVAL = -30000.0
NSLOT, NDEC = 9, 6
NS = NSLOT + NDEC * 2


def _rope_tables():
    inv_freq = 1.0 / 10000.0 ** (np.arange(0, RD, 2, dtype=np.float32) / RD)
    t = np.arange(T, dtype=np.float32)
    freqs = np.outer(t, inv_freq)
    emb = np.concatenate([freqs, freqs], -1)
    return np.cos(emb).astype(np.float32), np.sin(emb).astype(np.float32)


def make_trig(c):
    cos, sin = _rope_tables()
    pos = np.concatenate([np.arange(c * CH, (c + 1) * CH),
                          np.arange((7 - c) * CH, (8 - c) * CH)])
    ct = cos[pos].T
    st = sin[pos].T.copy()
    st[:RD // 2] *= -1.0
    return np.concatenate([np.tile(ct, (4, 1)), np.tile(st, (4, 1))],
                          axis=1).astype(np.float32)


def make_sched(c):
    A, Bc = c, 7 - c
    slots = [(A, A), (Bc, Bc), (Bc, A)]
    rest = [(A, k) for k in range(A)] + [(Bc, k) for k in range(Bc) if k != A]
    assert len(rest) == 6
    slots += rest
    vals = []
    for (qc, kc) in slots:
        vals.append(0 if qc == A else 1)
    for (_, kc) in rest:
        r = kc if kc <= 3 else 7 - kc
        sub = 0 if kc <= 3 else 1
        vals += [r * 768, sub * CH]
    assert len(vals) == NS
    return np.array([vals], dtype=np.int32)


def make_trimask():
    m = np.zeros((128, 128), dtype=np.float32)
    m[np.arange(128)[:, None] > np.arange(128)[None, :]] = MASKVAL
    return m


class TK:
    NL = 8

    def __init__(self, sems):
        self.sems = sems
        self.count = {k: 0 for k in sems}
        self.step = {k: (16 if k[0] == 'q' else 1) for k in sems}
        self.uses = {'qsy': 0, 'qgp': 0, 'qsc': 0}
        self.writer = {}
        self.readers = {}
        self.pending = {}
        self.seen = {}

    def _seen(self, who):
        return self.seen.setdefault(who, {})

    def pre(self, who, reads, writes):
        need = {}

        def w_need(w):
            if w is None:
                return
            if w[0] != who_self:
                need[w[0]] = max(need.get(w[0], 0), w[1])
        who_self = who
        for r in reads:
            w = self.writer.get(r)
            if w:
                need[w[0]] = max(need.get(w[0], 0), w[1])
        for r in writes:
            w = self.writer.get(r)
            if w:
                need[w[0]] = max(need.get(w[0], 0), w[1])
            for rs, rv in self.readers.get(r, {}).items():
                need[rs] = max(need.get(rs, 0), rv)
        out = {}
        seen = self._seen(who)
        for s, v in need.items():
            if s == who == 'pe':
                continue
            if seen.get(s, 0) < v:
                out[s] = v
                seen[s] = v
        return out

    def dma_lane(self, q, units=1):
        lane = self.uses[q] % TK.NL
        self.uses[q] += 1
        name = f'{q}{lane}'
        prev = self.count[name]
        self.count[name] = prev + 16 * units
        return name, prev, self.count[name]

    def post_compute(self, dom, reads, writes, inc):
        if inc:
            self.count[dom] += 1
        cnt = self.count[dom]
        pend = self.pending.setdefault(dom, {'r': set(), 'w': set()})
        for r in writes:
            self.writer[r] = (dom, cnt)
            self.readers[r] = {}
        for r in reads:
            self.readers.setdefault(r, {})[dom] = cnt
        if inc:
            for r in pend['r']:
                if self.readers.setdefault(r, {}).get(dom, 0) < cnt:
                    self.readers[r][dom] = cnt
            for r in pend['w']:
                w = self.writer.get(r)
                if w and w[0] == dom and w[1] < cnt:
                    self.writer[r] = (dom, cnt)
            pend['r'].clear()
            pend['w'].clear()
        else:
            pend['r'].update(reads)
            pend['w'].update(writes)

    def post_dma(self, semname, cnt, reads, writes):
        for r in writes:
            self.writer[r] = (semname, cnt)
            self.readers[r] = {}
        for r in reads:
            self.readers.setdefault(r, {})[dom := semname] = cnt


def build_nc(debug=False, upto=9):
    nc = bass.Bass("TRN2", target_bir_lowering=False)

    xin = nc.dram_tensor("xin", [D, TOWN], F32, kind="ExternalInput").ap()
    trig_d = nc.dram_tensor("trig", [128, 2 * TOWN], F32, kind="ExternalInput").ap()
    sched_d = nc.dram_tensor("sched", [1, NS], I32, kind="ExternalInput").ap()
    wkv_d = nc.dram_tensor("wkv", [D, KVR], F32, kind="ExternalInput").ap()
    gkv_d = nc.dram_tensor("gkv", [1, KVR], F32, kind="ExternalInput").ap()
    wkdec_d = nc.dram_tensor("wkdec", [KVR, D], F32, kind="ExternalInput").ap()
    wvdec_d = nc.dram_tensor("wvdec", [KVR, D], F32, kind="ExternalInput").ap()
    wq_d = nc.dram_tensor("wq", [D, QR], F32, kind="ExternalInput").ap()
    wqdec_d = nc.dram_tensor("wqdec", [QR, D], F32, kind="ExternalInput").ap()
    wkrope_d = nc.dram_tensor("wkrope", [D, QR], F32, kind="ExternalInput").ap()
    wqrope_d = nc.dram_tensor("wqrope", [QR, QR], F32, kind="ExternalInput").ap()
    wo_d = nc.dram_tensor("wo", [D, D], F32, kind="ExternalInput").ap()
    wupT_d = nc.dram_tensor("wupT", [D, DFF], F32, kind="ExternalInput").ap()
    wdownT_d = nc.dram_tensor("wdownT", [DFF, D], F32, kind="ExternalInput").ap()
    xout = nc.dram_tensor("xout", [D, TOWN], F32, kind="ExternalOutput").ap()
    dbg = {}
    if debug:
        for name, shp, dt_ in [("d_x1", [D, TOWN], F32),
                               ("d_lat", [768, TOWN], BF16),
                               ("d_qst", [NH * QDIM, TOWN], BF16),
                               ("d_kst", [NH * QDIM, 8 * CH], BF16),
                               ("d_vst", [8 * NH * 128, 520], BF16),
                               ("d_acc", [2 * NH * 65, CH], F32),
                               ("d_ao", [128, 16 * CH], BF16),
                               ("d_x2", [D, TOWN], F32),
                               ("d_wup", [16 * 8 * 128, 256], BF16),
                               ("d_wdn", [4 * 4 * 8 * 128, 256], BF16),
                               ("d_xn", [128, 4096], BF16),
                               ("d_h", [128, 16 * CH], BF16),
                               ("d_fin", [128, CH], F32),
                               ("d_st", [1, 2048], F32),
                               ("d_pt", [128, 2 * CH], BF16)]:
            dbg[name] = nc.dram_tensor(name, shp, dt_, kind="ExternalOutput").ap()

    lat_own = nc.dram_tensor("lat_own", [768, TOWN], BF16)
    lat_gath = nc.dram_tensor("lat_gath", [4 * 768, TOWN], BF16)
    wup_t = nc.dram_tensor("wup_t", [16 * 8 * 128, 256], BF16)
    wdn_t = nc.dram_tensor("wdn_t", [4 * 4 * 8 * 128, 256], BF16)
    kst = nc.dram_tensor("kst", [NH * QDIM, 8 * CH], BF16)
    vst = nc.dram_tensor("vst", [8 * NH * 128, 520], BF16)
    qst = nc.dram_tensor("qst", [NH * QDIM, TOWN], BF16)
    accd = nc.dram_tensor("accd", [2 * NH * 65, CH], F32)
    accd2 = nc.dram_tensor("accd2", [NSLOT * NH * 65, CH], F32)
    latslot = nc.dram_tensor("latslot", [NDEC * 768, CH], BF16)
    qstslot = nc.dram_tensor("qstslot", [NSLOT * NH * QDIM, CH], BF16)
    rpst = nc.dram_tensor("rpst", [128, TOWN], F32)

    tri_t = nc.inline_tensor(make_trimask(), "trimask")
    onesb_t = nc.inline_tensor(np.ones((128, 1), dtype=ml_dtypes.bfloat16), "onesbt")
    onesf_t = nc.inline_tensor(np.ones((1, 128), dtype=np.float32), "onesft")

    from contextlib import ExitStack
    with ExitStack() as _es:
        x_sb = _es.enter_context(nc.sbuf_tensor("x_s", [128, 8 * TOWN], F32))
        arA = _es.enter_context(nc.sbuf_tensor("arenaA", [128, 16 * CH], BF16))
        arB = _es.enter_context(nc.sbuf_tensor("arenaB", [128, 16 * CH], BF16))
        trig_sb = _es.enter_context(nc.sbuf_tensor("trig_s", [128, 2 * TOWN], BF16))
        fscr = _es.enter_context(nc.sbuf_tensor("fscr", [128, 4096], F32))
        stF = _es.enter_context(nc.sbuf_tensor("statsF", [1, 2048], F32))
        xn_sb = _es.enter_context(nc.sbuf_tensor("xn", [128, 4096], BF16))
        wstr = _es.enter_context(nc.sbuf_tensor("wstr", [128, 4096], BF16))
        ksb = _es.enter_context(nc.sbuf_tensor("ksb", [128, 2 * CH], BF16))
        vsb = _es.enter_context(nc.sbuf_tensor("vsb", [128, 2 * 520], BF16))
        qsb = _es.enter_context(nc.sbuf_tensor("qsb", [128, 2 * CH], BF16))
        pt_sb = _es.enter_context(nc.sbuf_tensor("pt", [128, 2 * CH], BF16))
        postg = _es.enter_context(nc.sbuf_tensor("postg", [65, 2 * CH], F32))
        fin_sb = _es.enter_context(nc.sbuf_tensor("fin", [128, 2 * CH], BF16))
        lrow = _es.enter_context(nc.sbuf_tensor("lrow", [1, CH], F32))
        tri_sb = _es.enter_context(nc.sbuf_tensor("tri", [128, 128], F32))
        onesb = _es.enter_context(nc.sbuf_tensor("onesb", [128, 1], BF16))
        onesf = _es.enter_context(nc.sbuf_tensor("onesf", [1, 128], F32))
        b25 = _es.enter_context(nc.sbuf_tensor("b25", [128, 1], F32))
        onesc = _es.enter_context(nc.sbuf_tensor("onesc", [128, 1], F32))
        gkv_sb = _es.enter_context(nc.sbuf_tensor("gkv_s", [128, 2], F32))
        sched_sb = _es.enter_context(nc.sbuf_tensor("schedsb", [1, NS], I32))
        psS0 = _es.enter_context(nc.psum_tensor("psS0", [128, CH], F32))
        psS1 = _es.enter_context(nc.psum_tensor("psS1", [128, CH], F32))
        psPO = _es.enter_context(nc.psum_tensor("psPO", [65, CH], F32))
        psX = _es.enter_context(nc.psum_tensor("psX", [128, 512], F32))
        s_pe = _es.enter_context(nc.semaphore("s_pe"))
        s_act = _es.enter_context(nc.semaphore("s_act"))
        s_dve = _es.enter_context(nc.semaphore("s_dve"))
        s_cc = _es.enter_context(nc.semaphore("s_cc"))
        qlanes = {}
        for q in ('qsy', 'qgp', 'qsc'):
            for ln_ in range(8):
                qlanes[f'{q}{ln_}'] = _es.enter_context(
                    nc.semaphore(f's_{q}{ln_}'))
        block = _es.enter_context(nc.Block())

        fin32 = fin_sb.ap().bitcast(F32)

        sems = {'pe': s_pe, 'act': s_act, 'dve': s_dve}
        sems.update(qlanes)
        tk = TK(sems)
        streams = {'pe': [], 'act': [], 'dve': [], 'sync': [], 'pool': []}
        D2S = {'pe': 'pe', 'act': 'act', 'dve': 'dve', 'qsy': 'sync',
               'qgp': 'pool', 'qsc': 'act'}

        def op(dom, fn, reads=(), writes=(), inc=True, samedep=True, tag="",
               nd=1):
            is_dma = dom[0] == 'q'
            waits = tk.pre(dom, list(reads), list(writes))
            if is_dma:
                semname, prev, cnt = tk.dma_lane(dom, nd)
                if prev > 0 and tk._seen(dom).get(semname, 0) < prev:
                    waits[semname] = max(waits.get(semname, 0), prev)
                    tk._seen(dom)[semname] = prev
                tk.post_dma(semname, cnt, list(reads), list(writes))
            else:
                if dom in ('dve', 'act'):
                    inc = True
                tk.post_compute(dom, list(reads), list(writes), inc)
                semname = dom

            def run(eng, waits=waits, fn=fn, inc=inc, is_dma=is_dma,
                    semname=semname, tag=tag, nd=nd):
                for s, v in waits.items():
                    eng.wait_ge(sems[s], v)
                inst = fn(eng, sems[semname]) if nd > 1 else fn(eng)
                if is_dma:
                    inst.then_inc(sems[semname], 16)
                elif inc:
                    assert inst is not None, f"op {tag} returned None"
                    inst.then_inc(sems[semname], 1)
            streams[D2S[dom]].append(run)

        def dma(dom, dst, src, reads=(), writes=(), accum=False, samedep=True,
                slow=False, tag=""):
            kw = {}
            if accum:
                kw['accum_op'] = ALU.add
            if slow:
                kw['allow_slow_non_contiguous'] = True
            op(dom, lambda e: e.dma_start(out=dst, in_=src, **kw),
               reads, writes, samedep=samedep, tag=tag)

        def lane(i):
            return stF[0:1, i * 512:(i + 1) * 512]

        SCHED_CNT = [None]

        def load_idx(eng, eng_ty, idx, maxv):
            eng.wait_ge(sems[SCHED_CNT[0][0]], SCHED_CNT[0][1])
            return nc.values_load(
                sched_sb[0:1, idx:idx + 1], engines=[eng_ty],
                min_val=0, max_val=maxv, skip_runtime_bounds_check=True)

        FREE_STATS = {'ok': 0, 'fail': 0}

        def free_val(*vals):
            for v in vals:
                try:
                    for hreg in v.val.handles:
                        nc.free_register(hreg)
                    FREE_STATS['ok'] += 1
                except Exception as ex:
                    FREE_STATS['fail'] += 1
                    if FREE_STATS['fail'] < 3:
                        print("free_val fail:", type(ex).__name__, ex)
        nc._FREE_STATS = FREE_STATS

        dma('qgp', x_sb[:, :].rearrange("p (c t) -> p c t", c=8),
            xin.rearrange("(c p) t -> p c t", p=128),
            writes=[f'x_{dc}' for dc in range(8)], tag="ldx")
        dma('qgp', trig_sb[:, :], trig_d, writes=['trig'])
        dma('qsy', tri_sb[:, :], tri_t.ap(), writes=['tri'])
        dma('qsy', onesb[:, :], onesb_t.ap(), writes=['onesb'])
        dma('qsy', onesf[:, :], onesf_t.ap(), writes=['onesf'])
        dma('qsy', sched_sb[:, :], sched_d, writes=['sched'])
        SCHED_CNT[0] = tk.writer['sched']
        for kc in range(2):
            dma('qsy', gkv_sb[:, kc:kc + 1],
                gkv_d[0:1, kc * 128:(kc + 1) * 128].rearrange("o p -> p o"),
                writes=['gkv'], slow=True)
        op('dve', lambda e: e.memset(b25[:, :], ESHIFT), writes=['b25'])
        op('dve', lambda e: e.memset(onesc[:, :], 1.0), writes=['onesf'])

        def tern_A(wsrc, nrow, ncol, key, qload, rbase, pcol):
            nrch, hcw = nrow // 128, 1024
            nhc = ncol // hcw
            npart = nrch * nhc
            lnk = f'ln{rbase // 2}'
            lcell = lane(rbase // 2)
            chunks = []
            for rc in range(nrch):
                for hc in range(nhc):
                    i = rc * nhc + hc
                    par = i % 2
                    sl = fscr[:, (rbase + par) * hcw:(rbase + par + 1) * hcw]
                    fsk = [f'fs{2 * (rbase + par)}', f'fs{2 * (rbase + par) + 1}']
                    rw = f'{key}a{i}'
                    chunks.append((i, par, sl, fsk, rw,
                                   wsrc[rc * 128:(rc + 1) * 128,
                                        hc * hcw:(hc + 1) * hcw]))

            def emit_chunk(n):
                i, par, sl, fsk, rw, src = chunks[n]
                dma(qload, sl, src, writes=[rw] + fsk, tag=f"{key}lda")
                pr = fin32[:, pcol + i:pcol + i + 1]
                op('dve', lambda e, sl=sl, pr=pr: e.tensor_reduce(
                    pr, sl, AX, ALU.add, apply_absolute_value=True),
                    reads=[rw] + fsk,
                    writes=[f'{key}p{i}', 'finbuf'])

            def emit_alpha():
                prs = fin32[:, pcol:pcol + npart]
                op('pe', lambda e, prs=prs, npart=npart: e.matmul(
                    psX[0:1, 0:npart], onesc[:, 0:1], prs,
                    start=True, stop=True, skip_group_check=True),
                   reads=[f'{key}p{i}' for i in range(npart)] +
                         ['onesf', 'finbuf'],
                   writes=['X'], tag=f"{key}amm")
                op('dve', lambda e, npart=npart: e.tensor_reduce(
                    lcell[0:1, 0:1], psX[0:1, 0:npart], AX, ALU.add),
                   reads=['X'], writes=[f'{key}al', lnk])
                op('dve', lambda e: e.tensor_scalar(
                    lcell[0:1, 1:2], lcell[0:1, 0:1],
                    0.5 / (nrow * ncol), None, ALU.mult),
                   reads=[f'{key}al', lnk], writes=[f'{key}th', lnk])
                op('pe', lambda e: e.matmul(
                    psX[:, 0:1], onesf[0:1, :], lcell[0:1, 1:2],
                    start=True, stop=True, skip_group_check=True),
                   reads=[f'{key}th', 'onesf', lnk], writes=['X'])
                thb = fin32[:, pcol + npart:pcol + npart + 1]
                op('dve', lambda e, thb=thb: e.tensor_copy(thb, psX[:, 0:1]),
                   reads=['X'], writes=[f'{key}thb', 'finbuf'])
            return chunks, emit_chunk, emit_alpha, npart

        def tern_B(key, qload, rbase, pcol, npart, chunks, store_fn, nhc):
            thb = fin32[:, pcol + npart:pcol + npart + 1]
            hcw = 1024

            def emit_chunk(n):
                i, par, sl, fsk, _rw, src = chunks[n]
                rc, hc = i // nhc, i % nhc
                rw = f'{key}b{i}'
                dma(qload, sl, src, reads=[f'{key}thb'], writes=[rw] + fsk,
                    tag=f"{key}ldb")
                sg = arA[:, (rbase + par) * hcw:(rbase + par + 1) * hcw]
                gt = pt_sb[:, par * hcw:(par + 1) * hcw] if rbase == 0 else \
                    qsb[:, par * hcw:(par + 1) * hcw]
                gk = f'pt{par}' if rbase == 0 else f'qsb{par}'
                op('act', lambda e, sl=sl, sg=sg: e.activation(
                    sg, sl, AF.Sign),
                   reads=[rw] + fsk,
                   writes=[f'{key}sg{i}', f'sg{rbase + par}'])
                op('act', lambda e, sl=sl: e.activation(sl, sl, AF.Abs),
                   reads=[rw, f'{key}sg{i}'] + fsk, writes=[rw] + fsk)
                op('dve', lambda e, sl=sl, gt=gt, thb=thb:
                   e.tensor_scalar(gt, sl, thb, None, ALU.is_gt),
                   reads=[rw, f'{key}thb', 'finbuf'] + fsk,
                   writes=[f'{key}gt{i}', gk])
                op('dve', lambda e, sg=sg, gt=gt: e.tensor_tensor(
                    sg, sg, gt, ALU.mult),
                   reads=[f'{key}gt{i}', f'{key}sg{i}', gk],
                   writes=[f'{key}wt{i}', f'sg{rbase + par}'])
                store_fn(rc, hc, sg, [f'{key}wt{i}', f'sg{rbase + par}', gk])
            return emit_chunk

        wup_v = wup_t.ap().rearrange("(f d p) c -> f d p c", f=16, d=8, p=128)
        wdn_v = wdn_t.ap().rearrange("(o q e p) c -> o q e p c", o=4, q=4,
                                     e=8, p=128)

        def store_up(rc, hc, src, reads):
            dst = wup_v[hc * 4: hc * 4 + 4, rc, :, :]
            sl = src.rearrange("p (f c) -> p f c", f=4)
            dma('qgp', dst.transpose([1, 0, 2]), sl, reads=reads,
                writes=[f'wupt_{hc * 4 + f}_{rc}' for f in range(4)],
                tag="stup")

        def store_dn(rc, hc, src, reads):
            fcq, e8 = rc // 8, rc % 8
            dst = wdn_v[:, fcq, e8, :, :]
            sl = src.rearrange("p (o c) -> p o c", o=4)
            dma('qgp', dst.transpose([1, 0, 2]), sl, reads=reads,
                writes=[f'wdnt_{rc}'], tag="stdn")

        if upto >= 1:
            Uch, Ua_emit, Ua_alpha, Unp = tern_A(wupT_d, D, DFF, 'U', 'qsy',
                                                 0, 0)
            Vch, Va_emit, Va_alpha, Vnp = tern_A(wdownT_d, DFF, D, 'V', 'qsc',
                                                 2, 40)
            for n in range(32):
                Ua_emit(n)
                Va_emit(n)
            Ua_alpha()
            Va_alpha()
            Ub_emit = tern_B('U', 'qsy', 0, 0, Unp, Uch, store_up, 4)
            Vb_emit = tern_B('V', 'qsc', 2, 40, Vnp, Vch, store_dn, 1)
            for n in range(32):
                Ub_emit(n)
                Vb_emit(n)

            def emit_ln(t0, pref, with_rms):
                for dc in range(8):
                    xs = x_sb[:, dc * TOWN + t0: dc * TOWN + t0 + 512]
                    sq = xn_sb[:, dc * 512:(dc + 1) * 512]
                    op('act', lambda e, xs=xs, sq=sq: e.activation(
                        sq, xs, AF.Square),
                       reads=[f'x_{dc}'], writes=[f'{pref}x2_{dc}', 'xnbuf'])
                for dc in range(8):
                    xs = x_sb[:, dc * TOWN + t0: dc * TOWN + t0 + 512]
                    op('pe', lambda e, xs=xs, dc=dc: e.matmul(
                        psX[0:1, :], onesc[:, 0:1], xs,
                        start=(dc == 0), stop=(dc == 7), skip_group_check=True),
                       reads=[f'x_{dc}', 'onesf'], writes=['X'],
                       inc=(dc == 7), tag="snx")
                for dc in range(8):
                    sq = xn_sb[:, dc * 512:(dc + 1) * 512]
                    op('pe', lambda e, sq=sq, dc=dc: e.matmul(
                        psS0[0:1, 0:512], onesb[:, 0:1], sq,
                        start=(dc == 0), stop=(dc == 7), skip_group_check=True),
                       reads=[f'{pref}x2_{dc}', 'onesb', 'xnbuf'], writes=['S0a'],
                       inc=(dc == 7), tag="snx2")
                op('dve', lambda e: e.tensor_scalar(
                    lane(0), psX[0:1, 0:512], 1.0 / D, None, ALU.mult),
                   reads=['X'], writes=['L_m', 'ln0'])
                op('dve', lambda e: e.tensor_scalar(
                    lane(1), psS0[0:1, 0:512], 1.0 / D, None, ALU.mult),
                   reads=['S0a'], writes=['L_mx2', 'ln1'])
                op('dve', lambda e: e.tensor_tensor(
                    lane(2), lane(0), lane(0), ALU.mult),
                   reads=['L_m', 'ln0'], writes=['L_m2', 'ln2'])
                op('dve', lambda e: e.tensor_tensor(
                    lane(1), lane(1), lane(2), ALU.subtract),
                   reads=['L_mx2', 'L_m2', 'ln1', 'ln2'], writes=['L_v', 'ln1'])
                op('dve', lambda e: e.tensor_scalar(
                    lane(2), lane(1), 1e-5, None, ALU.add),
                   reads=['L_v', 'ln1'], writes=['L_u', 'ln2'])
                op('act', lambda e: e.activation(lane(3), lane(2), AF.Sqrt),
                   reads=['L_u', 'ln2'], writes=['L_su', 'ln3'])
                op('dve', lambda e: e.reciprocal(lane(2), lane(3)),
                   reads=['L_su', 'ln3'], writes=['L_rstd', 'ln2'])
                if with_rms:
                    op('dve', lambda e: e.tensor_tensor(
                        lane(3), lane(1), lane(2), ALU.mult),
                       reads=['L_v', 'L_rstd', 'ln1', 'ln2'], writes=['L_w1', 'ln3'])
                    op('dve', lambda e: e.tensor_tensor(
                        lane(3), lane(3), lane(2), ALU.mult),
                       reads=['L_w1', 'L_rstd', 'ln3', 'ln2'], writes=['L_w', 'ln3'])
                    op('act', lambda e: e.activation(lane(1), lane(3), AF.Sqrt),
                       reads=['L_w', 'ln3'], writes=['L_sw', 'ln1'])
                    op('dve', lambda e: e.tensor_scalar(
                        lane(1), lane(1), 1e-8, None, ALU.add),
                       reads=['L_sw', 'ln1'], writes=['L_den', 'ln1'])
                    op('dve', lambda e: e.reciprocal(lane(1), lane(1)),
                       reads=['L_den', 'ln1'], writes=['L_rden', 'ln1'])
                    op('dve', lambda e: e.tensor_tensor(
                        lane(3), lane(2), lane(1), ALU.mult),
                       reads=['L_rstd', 'L_rden', 'ln2', 'ln1'], writes=['L_c', 'ln3'])
                    ckey, cl = 'L_c', 3
                else:
                    ckey, cl = 'L_rstd', 2
                op('dve', lambda e, cl=cl: e.tensor_tensor(
                    lane(1), lane(0), lane(cl), ALU.mult),
                   reads=['L_m', ckey, 'ln0'], writes=['L_mc', 'ln1'])
                op('pe', lambda e, cl=cl: e.matmul(
                    psS0[:, 0:512], onesf[0:1, :], lane(cl),
                    start=True, stop=True, skip_group_check=True),
                   reads=[ckey, 'onesf'], writes=['S0a'], tag="bc_c")
                op('pe', lambda e: e.matmul(
                    psS0[:, 512:1024], onesf[0:1, :], lane(1),
                    start=True, stop=True, skip_group_check=True),
                   reads=['L_mc', 'onesf', 'ln1'], writes=['S0b'], tag="bc_mc")

            def emit_xnorm(t0, pref, dst_fn, extra_w=()):
                for dc in range(8):
                    xs = x_sb[:, dc * TOWN + t0: dc * TOWN + t0 + 512]
                    tmp = fin32[:, (dc % 2) * 512:(dc % 2) * 512 + 512]
                    xo = dst_fn(dc)
                    op('dve', lambda e, xs=xs, tmp=tmp: e.tensor_tensor(
                        tmp, xs, psS0[:, 0:512], ALU.mult),
                       reads=[f'x_{dc}', 'S0a'], writes=[f'{pref}t1_{dc}',
                                                         'finbuf'])
                    op('dve', lambda e, tmp=tmp, xo=xo: e.tensor_tensor(
                        xo, tmp, psS0[:, 512:1024], ALU.subtract),
                       reads=[f'{pref}t1_{dc}', 'S0b'],
                       writes=[f'{pref}xn_{dc}', 'xnbuf'] + list(extra_w))

            def emit_ffn(ff):
                for tt in range(4):
                    t0 = tt * 512
                    emit_ln(t0, ff, with_rms=True)
                    emit_xnorm(t0, ff, lambda dc: xn_sb[:, dc * 512:(dc + 1) * 512])
                    for fcg in range(16):
                        wpar = fcg % 2
                        wsl = wstr[:, wpar * 2048:(wpar + 1) * 2048]
                        src = wup_v[fcg].transpose([1, 0, 2])
                        wdep = [f'wupt_{fcg}_{rc}' for rc in range(8)]
                        dma('qsy', wsl.rearrange("p (d c) -> p d c", d=8), src,
                            reads=wdep, writes=[f'wup{wpar}', f'wstr{wpar}'], tag="ldwup")
                        for fcw in range(2):
                            fc = fcg * 2 + fcw
                            pso = psS1[:, (fc % 2) * 512:(fc % 2) * 512 + 512]
                            pkey = 'S1a' if fc % 2 == 0 else 'S1b'
                            for dc in range(8):
                                lh = wsl[:, dc * 256 + fcw * 128:
                                         dc * 256 + fcw * 128 + 128]
                                op('pe', lambda e, lh=lh, dc=dc, pso=pso: e.matmul(
                                    pso, lh, xn_sb[:, dc * 512:(dc + 1) * 512],
                                    start=(dc == 0), stop=(dc == 7),
                                    skip_group_check=True),
                                   reads=[f'wup{wpar}', f'wstr{wpar}', f'{ff}xn_{dc}', 'xnbuf'],
                                   writes=[pkey], inc=(dc == 7), tag="upmm")
                            hsl = arB[:, fc * 512:(fc + 1) * 512]
                            op('act', lambda e, pso=pso, hsl=hsl: e.activation(
                                hsl, pso, AF.Gelu),
                               reads=[pkey], writes=[f'{ff}h_{fc}'], tag="gelu")
                    for fc in range(32):
                        hsl = arB[:, fc * 512:(fc + 1) * 512]
                        h2 = fscr[:, 1024 + (fc % 2) * 256:
                                  1024 + (fc % 2) * 256 + 256].bitcast(BF16)
                        op('dve', lambda e, hsl=hsl, h2=h2: e.tensor_tensor(
                            h2, hsl, hsl, ALU.mult),
                           reads=[f'{ff}h_{fc}'], writes=[f'{ff}h2_{fc}', 'fs2'])
                        op('pe', lambda e, h2=h2, fc=fc: e.matmul(
                            psX[0:1, 0:512], onesb[:, 0:1], h2,
                            start=(fc == 0), stop=(fc == 31), skip_group_check=True),
                           reads=[f'{ff}h2_{fc}', 'onesb', 'fs2'], writes=['X'],
                           inc=True, tag="snh2")
                    op('dve', lambda e: e.tensor_scalar(
                        lane(0), psX[0:1, 0:512], 1.0 / DFF, None, ALU.mult),
                       reads=['X'], writes=['H_ms', 'ln0'])
                    op('act', lambda e: e.activation(lane(1), lane(0), AF.Sqrt),
                       reads=['H_ms', 'ln0'], writes=['H_rms', 'ln1'])
                    op('dve', lambda e: e.tensor_scalar(
                        lane(1), lane(1), 1e-8, None, ALU.add),
                       reads=['H_rms', 'ln1'], writes=['H_den', 'ln1'])
                    op('dve', lambda e: e.reciprocal(lane(2), lane(1)),
                       reads=['H_den', 'ln1'], writes=['H_c2', 'ln2'])
                    op('pe', lambda e: e.matmul(
                        psX[:, 0:512], onesf[0:1, :], lane(2),
                        start=True, stop=True, skip_group_check=True),
                       reads=['H_c2', 'onesf', 'ln2'], writes=['X'], tag="bc_c2")
                    c2sb = fscr[:, 512:768].bitcast(BF16)
                    op('dve', lambda e, c2sb=c2sb: e.tensor_copy(
                        c2sb, psX[:, 0:512]),
                       reads=['X'], writes=['H_c2sb', 'fs1'])
                    for fc in range(32):
                        hsl = arB[:, fc * 512:(fc + 1) * 512]
                        op('dve', lambda e, hsl=hsl, c2sb=c2sb: e.tensor_tensor(
                            hsl, hsl, c2sb, ALU.mult),
                           reads=['H_c2sb', 'fs1', f'{ff}h_{fc}'],
                           writes=[f'{ff}hn_{fc}'])
                    for ocg in range(4):
                        for fcq in range(4):
                            wpar = (ocg * 4 + fcq) % 2
                            wsl = wstr[:, wpar * 2048:(wpar + 1) * 2048]
                            src = wdn_v[ocg, fcq].transpose([1, 0, 2])
                            wdep = [f'wdnt_{fcq * 8 + e8}' for e8 in range(8)]
                            dma('qsy', wsl.rearrange("p (d c) -> p d c", d=8), src,
                                reads=wdep, writes=[f'wdn{wpar}', f'wstr{wpar}'], tag="ldwdn")
                            for ocw in range(2):
                                pso = psS1[:, ocw * 512: ocw * 512 + 512]
                                pkey = 'S1a' if ocw == 0 else 'S1b'
                                for e8 in range(8):
                                    fc = fcq * 8 + e8
                                    lh = wsl[:, e8 * 256 + ocw * 128:
                                             e8 * 256 + ocw * 128 + 128]
                                    op('pe', lambda e, lh=lh, fc=fc, pso=pso,
                                       fcq=fcq, e8=e8: e.matmul(
                                           pso, lh,
                                           arB[:, fc * 512:(fc + 1) * 512],
                                           start=(fcq == 0 and e8 == 0),
                                           stop=(fcq == 3 and e8 == 7),
                                           skip_group_check=True),
                                       reads=[f'wdn{wpar}', f'wstr{wpar}', f'{ff}hn_{fc}'],
                                       writes=[pkey],
                                       inc=(e8 == 7), tag="dnmm")
                        for ocw in range(2):
                            oc = ocg * 2 + ocw
                            pso = psS1[:, ocw * 512: ocw * 512 + 512]
                            pkey = 'S1a' if ocw == 0 else 'S1b'
                            xs = x_sb[:, oc * TOWN + t0: oc * TOWN + t0 + 512]
                            op('dve', lambda e, pso=pso, xs=xs:
                               e.scalar_tensor_tensor(
                                   xs, pso, 0.5, xs, ALU.mult, ALU.add),
                               reads=[pkey, f'x_{oc}'], writes=[f'x_{oc}'],
                               tag="resid")

            emit_ffn('F1')
            if debug:
                dma('qsy', dbg["d_wup"], wup_t.ap(),
                    reads=[f'wupt_{f}_{rc}' for f in range(16) for rc in range(8)],
                    writes=['dbgwu'])
                dma('qsy', dbg["d_wdn"], wdn_t.ap(),
                    reads=[f'wdnt_{rc}' for rc in range(32)], writes=['dbgwd'])
                dma('qsy', dbg["d_xn"], xn_sb[:, :],
                    reads=[f'F1xn_{dc}' for dc in range(8)], writes=['dbgxn'])
                dma('qsy', dbg["d_h"], arB[:, :],
                    reads=[f'F1hn_{fc}' for fc in range(32)], writes=['dbgh'])
            if debug:
                dma('qsy', dbg["d_x1"].rearrange("(c p) t -> p c t", p=128),
                    x_sb[:, :].rearrange("p (c t) -> p c t", c=8),
                    reads=[f'x_{dc}' for dc in range(8)], writes=['dbg1'])
        if upto >= 2:

            for tt in range(4):
                emit_ln(tt * 512, f'P{tt}', with_rms=False)
                emit_xnorm(tt * 512, f'P{tt}',
                           lambda dc, tt=tt: arA[:, dc * TOWN + tt * 512:
                                                 dc * TOWN + tt * 512 + 512],
                           extra_w=('sg0', 'sg1', 'sg2', 'sg3'))
            xlreads = [f'P{tt}xn_{dc}' for dc in range(8) for tt in range(4)]

            def proj(wsrc, nin, nout, rhs_arena, rhs_reads, out_fn, wkey,
                     scale=None, ps=None, pk=('S1a', 'S1b')):
                ps = psS1 if ps is None else ps
                nic, noc = nin // 128, nout // 128
                for oc in range(noc):
                    wpar = oc % 2
                    wsl = wstr[:, wpar * 2048:(wpar + 1) * 2048]
                    src = wsrc.rearrange("(i p) o -> p i o", p=128)[
                        :, :, oc * 128:(oc + 1) * 128]
                    dma('qgp', wsl[:, 0:nic * 128].rearrange(
                        "p (i c) -> p i c", i=nic), src,
                        writes=[f'{wkey}{wpar}', f'wstr{wpar}'], tag=f"ld{wkey}")
                    if scale is not None:
                        op('dve', lambda e, wsl=wsl, nic=nic: e.tensor_scalar(
                            wsl[:, 0:nic * 128], wsl[:, 0:nic * 128], scale,
                            None, ALU.mult),
                           reads=[f'{wkey}{wpar}'], writes=[f'{wkey}{wpar}'],
                           tag=f"sc{wkey}")
                    for tt in range(4):
                        pso = ps[:, (tt % 2) * 512:(tt % 2) * 512 + 512]
                        pkey = pk[0] if tt % 2 == 0 else pk[1]
                        for ic in range(nic):
                            lh = wsl[:, ic * 128:(ic + 1) * 128]
                            op('pe', lambda e, lh=lh, ic=ic, tt=tt, pso=pso,
                               nic=nic: e.matmul(
                                   pso, lh,
                                   rhs_arena[:, ic * TOWN + tt * 512:
                                             ic * TOWN + tt * 512 + 512],
                                   start=(ic == 0), stop=(ic == nic - 1),
                                   skip_group_check=True),
                               reads=[f'{wkey}{wpar}', f'wstr{wpar}'] + rhs_reads,
                               writes=[pkey], inc=(ic == nic - 1),
                               tag=f"mm{wkey}")
                        out_fn(oc, tt, pso, [pkey])

            def out_qc(oc, tt, pso, reads):
                dst = arB[:, oc * TOWN + tt * 512: oc * TOWN + tt * 512 + 512]
                op('dve', lambda e, pso=pso, dst=dst: e.tensor_copy(dst, pso),
                   reads=reads, writes=[f'qc_{oc}_{tt}'], tag="evqc")
            qcreads = [f'qc_{oc}_{tt}' for oc in range(4) for tt in range(4)]

            KVB = 12 * CH

            def out_kv(oc, tt, pso, reads):
                dst = arB[:, KVB + oc * TOWN + tt * 512:
                          KVB + oc * TOWN + tt * 512 + 512]
                op('dve', lambda e, pso=pso, dst=dst: e.tensor_copy(dst, pso),
                   reads=reads, writes=[f'kvc_{oc}_{tt}'], tag="evkv")
            proj(wkv_d, D, KVR, arA, xlreads, out_kv, 'Wkv')
            for tt in range(4):
                for oc in range(2):
                    kvs = arB[:, KVB + oc * TOWN + tt * 512:
                              KVB + oc * TOWN + tt * 512 + 512]
                    sq = fscr[:, 1024 + oc * 256:1024 + oc * 256 + 256].bitcast(BF16)
                    op('dve', lambda e, kvs=kvs, sq=sq: e.tensor_tensor(
                        sq, kvs, kvs, ALU.mult),
                       reads=[f'kvc_{oc}_{tt}'], writes=[f'kv2_{oc}', 'fs2'])
                    op('pe', lambda e, sq=sq, oc=oc: e.matmul(
                        psX[0:1, 0:512], onesb[:, 0:1], sq,
                        start=(oc == 0), stop=(oc == 1), skip_group_check=True),
                       reads=[f'kv2_{oc}', 'onesb', 'fs2'], writes=['X'],
                       inc=(oc == 1), tag="snkv")
                op('dve', lambda e: e.tensor_scalar(
                    lane(0), psX[0:1, 0:512], 1.0 / KVR, None, ALU.mult),
                   reads=['X'], writes=['K_ms', 'ln0'])
                op('dve', lambda e: e.tensor_scalar(
                    lane(0), lane(0), 1e-6, None, ALU.add),
                   reads=['K_ms', 'ln0'], writes=['K_u', 'ln0'])
                op('act', lambda e: e.activation(lane(1), lane(0), AF.Sqrt),
                   reads=['K_u', 'ln0'], writes=['K_su', 'ln1'])
                op('dve', lambda e: e.reciprocal(lane(2), lane(1)),
                   reads=['K_su', 'ln1'], writes=['K_rs', 'ln2'])
                op('pe', lambda e: e.matmul(
                    psX[:, 0:512], onesf[0:1, :], lane(2),
                    start=True, stop=True, skip_group_check=True),
                   reads=['K_rs', 'onesf', 'ln2'], writes=['X'], tag="bc_krs")
                for oc in range(2):
                    kvs = arB[:, KVB + oc * TOWN + tt * 512:
                              KVB + oc * TOWN + tt * 512 + 512]
                    op('dve', lambda e, kvs=kvs: e.tensor_tensor(
                        kvs, kvs, psX[:, 0:512], ALU.mult),
                       reads=['X', f'kvc_{oc}_{tt}'], writes=[f'kvn1_{oc}_{tt}'])
                    op('dve', lambda e, kvs=kvs, oc=oc: e.tensor_scalar(
                        kvs, kvs, gkv_sb[:, oc:oc + 1], None, ALU.mult),
                       reads=[f'kvn1_{oc}_{tt}', 'gkv'],
                       writes=[f'kvn_{oc}_{tt}'])
            kvnreads = [f'kvn_{oc}_{tt}' for oc in range(2) for tt in range(4)]
            for oc in range(2):
                dma('qsy', lat_own.ap()[oc * 128:(oc + 1) * 128, :],
                    arB[:, KVB + oc * TOWN: KVB + (oc + 1) * TOWN],
                    reads=[f'kvn_{oc}_{tt}' for tt in range(4)],
                    writes=[f'lat_kv{oc}'], tag="stkvn")

            def emit_rope(wsrc, rhs_arena, rhs_reads, nin, out_fn, wkey,
                          scale=None):
                nic = nin // 128
                for kc4 in range(4):
                    wpar = kc4 % 2
                    wsl = wstr[:, wpar * 2048:(wpar + 1) * 2048]
                    src = wsrc.rearrange("(i p) o -> p i o", p=128)[
                        :, :, kc4 * 128:(kc4 + 1) * 128]
                    dma('qgp', wsl[:, 0:nic * 128].rearrange(
                        "p (i c) -> p i c", i=nic), src,
                        writes=[f'{wkey}{wpar}', f'wstr{wpar}'], tag=f"ld{wkey}")
                    if scale is not None:
                        op('dve', lambda e, wsl=wsl, nic=nic: e.tensor_scalar(
                            wsl[:, 0:nic * 128], wsl[:, 0:nic * 128], scale,
                            None, ALU.mult),
                           reads=[f'{wkey}{wpar}'], writes=[f'{wkey}{wpar}', f'wstr{wpar}'])
                    for tt in range(4):
                        pso = psS1[:, (tt % 2) * 512:(tt % 2) * 512 + 512]
                        pkey = 'S1a' if tt % 2 == 0 else 'S1b'
                        for ic in range(nic):
                            lh = wsl[:, ic * 128:(ic + 1) * 128]
                            rhs = rhs_arena[:, ic * TOWN + tt * 512:
                                            ic * TOWN + tt * 512 + 512]
                            op('pe', lambda e, lh=lh, rhs=rhs, ic=ic, pso=pso,
                               nic=nic: e.matmul(
                                   pso, lh, rhs, start=(ic == 0),
                                   stop=(ic == nic - 1), skip_group_check=True),
                               reads=[f'{wkey}{wpar}', f'wstr{wpar}'] + rhs_reads,
                               writes=[pkey], inc=(ic == nic - 1),
                               tag=f"mm{wkey}")
                        dst = fscr[:, tt * 512:(tt + 1) * 512]
                        op('dve', lambda e, pso=pso, dst=dst: e.tensor_copy(
                            dst, pso),
                           reads=[pkey], writes=[f'{wkey}ev_{tt}',
                                                 f'fs{tt}'],
                           tag=f"ev{wkey}")
                    evr = [f'{wkey}ev_{tt}' for tt in range(4)]
                    for g4 in range(4):
                        for hh in range(2):
                            dma('qsy',
                                fscr[g4 * 32 + hh * 16: g4 * 32 + hh * 16 + 16,
                                     TOWN:2 * TOWN],
                                fscr[g4 * 32 + (1 - hh) * 16:
                                     g4 * 32 + (1 - hh) * 16 + 16, 0:TOWN],
                                reads=evr + ['fs0', 'fs1', 'fs2', 'fs3'],
                                writes=[f'{wkey}rot{g4}_{hh}', 'fs4', 'fs5',
                                        'fs6', 'fs7'],
                                tag="ldrot")
                    rotr = [f'{wkey}rot{g4}_{hh}' for g4 in range(4)
                            for hh in range(2)]
                    a = fscr[:, 0:TOWN]
                    bb = fscr[:, TOWN:2 * TOWN]
                    op('dve', lambda e, bb=bb: e.tensor_tensor(
                        bb, bb, trig_sb[:, TOWN:2 * TOWN], ALU.mult),
                       reads=rotr + ['trig'],
                       writes=[f'{wkey}r1', 'fs4', 'fs5', 'fs6', 'fs7'])
                    op('dve', lambda e, a=a: e.tensor_tensor(
                        a, a, trig_sb[:, 0:TOWN], ALU.mult),
                       reads=evr + ['trig', f'{wkey}rp'],
                       writes=[f'{wkey}r2', 'fs0', 'fs1', 'fs2', 'fs3'])
                    op('dve', lambda e, a=a, bb=bb: e.tensor_tensor(
                        a, a, bb, ALU.add),
                       reads=[f'{wkey}r1', f'{wkey}r2'],
                       writes=[f'{wkey}rope_{kc4}', 'fs0', 'fs1', 'fs2', 'fs3'])
                    out_fn(kc4, a, [f'{wkey}rope_{kc4}', 'fs0', 'fs1', 'fs2',
                                'fs3'])

            def out_krope(kc4, src, reads):
                dma('qgp', lat_own.ap()[256 + kc4 * 128: 256 + (kc4 + 1) * 128, :],
                    src, reads=reads, writes=[f'lat_kr{kc4}'], tag="stkr")
            emit_rope(wkrope_d, arA, xlreads, D, out_krope, 'Wkr')

            gwaits = tk.pre('qgp', ['lat_kv0', 'lat_kv1'] +
                            [f'lat_kr{k}' for k in range(4)], [])

            def _gather(e, gwaits=gwaits):
                for s_, v_ in gwaits.items():
                    e.wait_ge(sems[s_], v_)
                e.collective_compute(
                    "AllGather", ALU.bypass,
                    replica_groups=[[0, 1, 2, 3], [4, 5, 6, 7]],
                    ins=[lat_own.ap().opt()],
                    outs=[lat_gath.ap().opt()]).then_inc(s_cc)
            streams['pool'].append(_gather)

            proj(wq_d, D, QR, arA, xlreads, out_qc, 'Wq',
                 ps=psS0, pk=('S0a', 'S0b'))

            def qrow_content(h):
                return h * 96

            def qrow_rope(h):
                return h * 96 + 64

            def out_qrope(kc4, src, reads):
                for j in range(4):
                    h = kc4 * 4 + j
                    dma('qgp', qst.ap()[qrow_rope(h): qrow_rope(h) + 32, :],
                        src[j * 32:(j + 1) * 32, :], reads=reads,
                        writes=[f'qst_r{h}'], tag="stqr")
            emit_rope(wqrope_d, arB, qcreads, QR, out_qrope, 'Wqr', scale=SCALE)

            def out_qdec(oc, tt, pso, reads):
                dst = xn_sb[:, tt * 512:(tt + 1) * 512]
                op('dve', lambda e, pso=pso, dst=dst: e.tensor_copy(dst, pso),
                   reads=reads, writes=[f'qd_{oc}_{tt}', 'xnbuf'], tag="evqd")
                if tt == 3:
                    for hh in range(2):
                        h = oc * 2 + hh
                        dma('qgp', qst.ap()[qrow_content(h): qrow_content(h) + 64, :],
                            xn_sb[hh * 64:(hh + 1) * 64, 0:TOWN],
                            reads=[f'qd_{oc}_{t}' for t in range(4)] + ['xnbuf'],
                            writes=[f'qst_c{h}'], tag="stqd")
            proj(wqdec_d, QR, D, arB, qcreads, out_qdec, 'Wqd', scale=SCALE,
                 ps=psS0, pk=('S0a', 'S0b'))
            qstreads = [f'qst_c{h}' for h in range(16)] + \
                       [f'qst_r{h}' for h in range(16)]

            if debug:
                dma('qsy', dbg["d_lat"], lat_own.ap(),
                    reads=['lat_kv0', 'lat_kv1'] + [f'lat_kr{k}' for k in range(4)],
                    writes=['dbgl'])
                dma('qsy', dbg["d_qst"], qst.ap(), reads=qstreads, writes=['dbgq'])

            dma('qgp', xn_sb[:, 0:2048].rearrange("p (i c) -> p i c", i=2),
                wkdec_d.rearrange("(i p) o -> p i o", p=128),
                writes=['Wkd', 'xnbuf'] + [f'qd_{oc}_{tt}' for oc in range(8)
                                           for tt in range(4)], tag="ldwkd")
            dma('qgp', xn_sb[:, 2048:4096].rearrange("p (i c) -> p i c", i=2),
                wvdec_d.rearrange("(i p) o -> p i o", p=128),
                writes=['Wvd', 'xnbuf'], tag="ldwvd")
            op('dve', lambda e: e.memset(
                arB[:, 0:8320].rearrange("p (h s c) -> p h s c", h=16, s=8)
                [:, :, :, 64:65], 1.0),
               reads=[f'Wqr{i}' for i in range(2)],
               writes=['vones', 'arBv'] + qcreads, tag="vones")

            def emit_dec(di):
                key = f'dec{di}'
                own = di < 2
                if own:
                    kv_rhs = [arB[:, KVB + oc * TOWN + di * CH:
                                  KVB + oc * TOWN + di * CH + CH]
                              for oc in range(2)]
                    kvreads = list(kvnreads)
                else:
                    j = di - 2
                    kvl = fscr[:, 0:1024].bitcast(BF16)
                    for i2 in range(2):
                        dma('qsy', kvl[:, i2 * CH:(i2 + 1) * CH],
                            latslot.ap()[j * 768 + i2 * 128:
                                         j * 768 + (i2 + 1) * 128, :],
                            reads=[f'lsl{j}'],
                            writes=[f'{key}kv', 'fs0', 'fs1'], tag="ldkvg")
                    kv_rhs = [kvl[:, 0:CH], kvl[:, CH:2 * CH]]
                    kvreads = [f'{key}kv', 'fs0', 'fs1']
                for h in range(NH):
                    oc, hh = h // 2, h % 2
                    for th in range(2):
                        pp = (h * 2 + th) % 2
                        pso = psS1[0:64, pp * 512: pp * 512 + 512]
                        pkey = 'S1a' if pp == 0 else 'S1b'
                        for ic in range(2):
                            lh = xn_sb[:, ic * 1024 + oc * 128 + hh * 64:
                                       ic * 1024 + oc * 128 + hh * 64 + 64]
                            op('pe', lambda e, lh=lh, pso=pso, ic=ic,
                               rhs=kv_rhs[ic], th=th: e.matmul(
                                   pso, lh, rhs[:, th * 512:(th + 1) * 512],
                                   start=(ic == 0), stop=(ic == 1),
                                   skip_group_check=True),
                               reads=['Wkd', 'xnbuf'] + kvreads, writes=[pkey],
                               inc=(ic == 1), tag="kdmm")
                        dst = arA[0:64, h * CH + th * 512: h * CH + th * 512 + 512]
                        xfence = xlreads if own else []
                        if h % 2 == 0:
                            op('dve', lambda e, pso=pso, dst=dst:
                               e.tensor_copy(dst, pso),
                               reads=[pkey], writes=[f'{key}kc_{h}_{th}',
                                                     'arAk'] + xfence, tag="kdev")
                        else:
                            op('act', lambda e, pso=pso, dst=dst:
                               e.copy(dst, pso),
                               reads=[pkey], writes=[f'{key}kc_{h}_{th}',
                                                     'arAk'] + xfence, tag="kdev")
                for h in range(NH):
                    dst = arA[64:96, h * CH:(h + 1) * CH]
                    if own:
                        src_ = lat_own.ap()[256 + h * 32: 256 + (h + 1) * 32,
                                            di * CH:(di + 1) * CH]
                        dma('qsy', dst, src_,
                            reads=[f'lat_kr{h // 4}'],
                            writes=[f'{key}kr_{h}', 'arAk'] + xlreads,
                            tag="ldkrow")
                    else:
                        dma('qsy', dst,
                            latslot.ap()[j * 768 + 256 + h * 32:
                                         j * 768 + 256 + (h + 1) * 32, :],
                            reads=[f'lsl{j}'],
                            writes=[f'{key}kr_{h}', 'arAk'], tag="ldkrg")
                kreads = [f'{key}kc_{h}_{th}' for h in range(16)
                          for th in range(2)] + \
                         [f'{key}kr_{h}' for h in range(16)]
                for h in range(NH):
                    dma('qsc', kst.ap()[h * 96:(h + 1) * 96,
                                        di * CH:(di + 1) * CH],
                        arA[0:96, h * CH:(h + 1) * CH],
                        reads=kreads + ['arAk'], writes=[f'{key}kst{h}'],
                        tag="stkst")
                for ts in range(8):
                    for dh in range(2):
                        pp = (ts * 2 + dh) % 2
                        pso = psS1[:, pp * 512: pp * 512 + 512]
                        pkey = 'S1a' if pp == 0 else 'S1b'
                        for ic in range(2):
                            lh = kv_rhs[ic][:, ts * 128:(ts + 1) * 128]
                            op('pe', lambda e, lh=lh, pso=pso, ic=ic, dh=dh:
                               e.matmul(
                                   pso, lh,
                                   xn_sb[:, 2048 + ic * 1024 + dh * 512:
                                         2048 + ic * 1024 + dh * 512 + 512],
                                   start=(ic == 0), stop=(ic == 1),
                                   skip_group_check=True),
                               reads=['Wvd', 'xnbuf'] + kvreads, writes=[pkey],
                               inc=(ic == 1), tag="vdmm")
                        dst = arB[:, 0:8320].rearrange(
                            "p (h s c) -> p h s c", h=16, s=8)[
                            :, dh * 8:(dh + 1) * 8, ts, 0:64]
                        op('dve', lambda e, pso=pso, dst=dst: e.tensor_copy(
                            dst, pso.rearrange("p (h c) -> p h c", h=8)),
                           reads=[pkey], writes=[f'{key}v_{ts}_{dh}', 'arBv'],
                           tag="vdev")
                vreads = [f'{key}v_{ts}_{dh}' for ts in range(8)
                          for dh in range(2)]
                for h in range(NH):
                    dma('qsc', vst.ap()[di * 2048 + h * 128:
                                        di * 2048 + (h + 1) * 128, :],
                        arB[:, h * 520:(h + 1) * 520],
                        reads=vreads + ['vones', 'arBv'],
                        writes=[f'{key}vst{h}'], tag="stvst")

            acc_v = accd.ap().rearrange("(q x) t -> q x t", q=2)

            def kst_reads(s):
                if s <= 2:
                    return [f'dec0kst{h}' for h in range(16)] + \
                           [f'dec1kst{h}' for h in range(16)]
                return [f'dec{2 + j}kst{h}' for j in range(s - 2) for h in range(16)]

            def vst_reads(s):
                if s <= 2:
                    return [f'dec0vst{h}' for h in range(16)] + \
                           [f'dec1vst{h}' for h in range(16)]
                return [f'dec{2 + j}vst{h}' for j in range(s - 2) for h in range(16)]

            def emit_slot(s, h):
                idx = s * 16 + h
                spar = idx % 2
                kb = 0
                diag = s < 2
                key = f'sl{s}_{h}'
                m = s if s < 2 else (0 if s == 2 else s - 1)

                dma('qsy', ksb[kb:kb + 96, spar * CH:(spar + 1) * CH],
                    kst.ap()[h * 96:(h + 1) * 96, m * CH:(m + 1) * CH],
                    reads=[f'dec{m}kst{h}'], writes=[f'ksb{spar}'], tag="ldK")
                dma('qsy', vsb[:, spar * 520:(spar + 1) * 520],
                    vst.ap()[m * 2048 + h * 128: m * 2048 + (h + 1) * 128, :],
                    reads=[f'dec{m}vst{h}'], writes=[f'vsb{spar}'], tag="ldV")
                dma('qsy', qsb[kb:kb + 96, spar * CH:(spar + 1) * CH],
                    qstslot.ap()[s * 1536 + h * 96: s * 1536 + (h + 1) * 96, :],
                    reads=[f'qsl{s}'], writes=[f'qsb{spar}'], tag="ldQ")

                def emit_pv(j, segs):
                    jpar = j % 2
                    lhv = vsb[:, spar * 520 + j * 65: spar * 520 + j * 65 + 65]
                    for si, (off, sw) in enumerate(segs):
                        op('pe', lambda e, off=off, sw=sw, lhv=lhv, jpar=jpar,
                           j=j: e.matmul(
                               psPO[:, off:off + sw], lhv,
                               pt_sb[:, jpar * CH + off: jpar * CH + off + sw],
                               start=(j == 0), stop=(j == 7),
                               skip_group_check=True),
                           reads=[f'vsb{spar}', f'pt{jpar}'], writes=['PO'],
                           inc=(si == len(segs) - 1), tag="pv")

                prev_pv = None
                for j in range(8):
                    jpar = j % 2
                    psS = psS0 if jpar == 0 else psS1
                    sk = 'S0' if jpar == 0 else 'S1'
                    skeys = [sk + 'a', sk + 'b']
                    w0 = j * 128 if diag else 0
                    wd = CH - w0
                    segs = []
                    soff = w0
                    while soff < CH:
                        slim = min((soff // 512 + 1) * 512, CH)
                        segs.append((soff, slim - soff))
                        soff = slim
                    lhk = ksb[kb:kb + 96, spar * CH + j * 128:
                              spar * CH + j * 128 + 128]
                    for si, (off, sw) in enumerate(segs):
                        op('pe', lambda e, off=off, sw=sw, lhk=lhk, psS=psS,
                           spar=spar, kb=kb: e.matmul(
                               psS[:, off:off + sw], lhk,
                               qsb[kb:kb + 96, spar * CH + off:
                                   spar * CH + off + sw],
                               start=True, stop=True, skip_group_check=True),
                           reads=[f'ksb{spar}', f'qsb{spar}'], writes=skeys,
                           inc=(si == len(segs) - 1), tag="qk")
                    if diag:
                        op('dve', lambda e, psS=psS, w0=w0: e.tensor_tensor(
                            psS[:, w0:w0 + 128], psS[:, w0:w0 + 128],
                            tri_sb[:, :], ALU.add),
                           reads=skeys + ['tri'], writes=skeys, tag="mask")
                    op('act', lambda e, psS=psS, w0=w0, wd=wd, jpar=jpar:
                       e.activation(
                           pt_sb[:, jpar * CH + w0: jpar * CH + w0 + wd],
                           psS[:, w0:w0 + wd], AF.Exp, bias=b25[:, 0:1],
                           scale=1.0),
                       reads=skeys + ['b25'], writes=[f'pt{jpar}'], tag="exp")
                    if prev_pv is not None:
                        emit_pv(*prev_pv)
                    prev_pv = (j, segs)
                emit_pv(*prev_pv)
                ppar = idx % 2
                op('dve', lambda e, ppar=ppar: e.tensor_copy(
                    postg[:, ppar * CH:(ppar + 1) * CH], psPO[:, :]),
                   reads=['PO'], writes=[f'pg{ppar}'], tag="poev")

                dma('qgp', accd2.ap()[(s * 16 + h) * 65:(s * 16 + h) * 65 + 65, :],
                    postg[:, ppar * CH:(ppar + 1) * CH],
                    reads=[f'pg{ppar}'], writes=[f'a2_{s}_{h}'],
                    samedep=False, tag="stacc")

            def emit_fin(h):
                key = f'fin{h}'
                rb = 0 if h % 2 == 0 else 64
                for q in range(2):
                    dma('qsy', fin32[rb:rb + 64, 0:CH],
                        accd.ap()[(q * 16 + h) * 65:(q * 16 + h) * 65 + 64, :],
                        reads=['accm'], writes=[f'{key}o{q}', 'finbuf'],
                        tag="ldfin")
                    dma('qsy', lrow[0:1, :],
                        accd.ap()[(q * 16 + h) * 65 + 64:(q * 16 + h) * 65 + 65, :],
                        reads=['accm'], writes=['lrow'], tag="ldl")
                    for half in range(2):
                        op('pe', lambda e, rb=rb, half=half: e.matmul(
                            psX[rb:rb + 64, 0:512], onesf[0:1, 0:64],
                            lrow[0:1, half * 512:(half + 1) * 512],
                            start=True, stop=True, skip_group_check=True,
                            tile_position=(0, rb)),
                           reads=['lrow', 'onesf'], writes=['X'], tag="bcl")
                        op('dve', lambda e, rb=rb: e.reciprocal(
                            psX[rb:rb + 64, 0:512], psX[rb:rb + 64, 0:512]),
                           reads=['X'], writes=['X'], tag="recl")
                        dst = arB[rb:rb + 64,
                                  (q * 8 + h // 2) * CH + half * 512:
                                  (q * 8 + h // 2) * CH + half * 512 + 512]
                        op('dve', lambda e, rb=rb, half=half, dst=dst, q=q:
                           e.tensor_tensor(
                               dst, fin32[rb:rb + 64,
                                          half * 512: half * 512 + 512],
                               psX[rb:rb + 64, 0:512], ALU.mult),
                           reads=[f'{key}o{q}', 'X', 'finbuf'],
                           writes=[f'ao_{q}_{h // 2}_{h % 2}', 'arBv'] +
                                  (kvnreads if q == 1 else []),
                           tag="findiv")

        if upto >= 2.2:
            qst_q = qst.ap().rearrange("r (q t) -> r q t", q=2)
            for s in range(NSLOT):
                dstq = qstslot.ap()[s * 1536:(s + 1) * 1536, :]
                if s < 3:
                    qv_static = [0, 1, 1][s]
                    dma('qsc', dstq, qst_q[:, qv_static, :],
                        reads=qstreads, writes=[f'qsl{s}'], tag="qslcp")
                else:
                    def qcp(e, s=s, dstq=dstq):
                        qv = load_idx(e, mybir.EngineType.Activation, s, 1)
                        inst = e.dma_start(
                            out=dstq,
                            in_=qst_q[:, bass.ds(qv, 1), :].squeeze(1))
                        free_val(qv)
                        return inst
                    op('qsc', qcp, reads=qstreads, writes=[f'qsl{s}'], tag="qslcp")

        if upto >= 2.4:
            emit_dec(0)
            emit_dec(1)

            def emit_merge(s):
                blk_src = accd2.ap()[s * 1040:(s + 1) * 1040, :]
                rd = [f'a2_{s}_{h}' for h in range(NH)]
                if s < 3:
                    qv_static = [0, 1, 1][s]
                    dst = accd.ap()[qv_static * 1040:(qv_static + 1) * 1040, :]
                    if s == 2:
                        for cc in range(2):
                            dma('qgp', dst[:, cc * 512:(cc + 1) * 512],
                                blk_src[:, cc * 512:(cc + 1) * 512],
                                reads=rd, writes=['accm'],
                                accum=True, tag="mrg")
                    else:
                        dma('qgp', dst, blk_src, reads=rd, writes=['accm'],
                            tag="mrg")
                else:
                    def mg(e, sem, s=s, blk_src=blk_src):
                        qv = load_idx(e, POOL, s, 1)
                        i1 = e.dma_start(
                            out=acc_v[bass.ds(qv, 1), :, 0:512].squeeze(0),
                            in_=blk_src[:, 0:512], accum_op=ALU.add)
                        i1.then_inc(sem, 16)
                        i2 = e.dma_start(
                            out=acc_v[bass.ds(qv, 1), :, 512:1024].squeeze(0),
                            in_=blk_src[:, 512:1024], accum_op=ALU.add)
                        free_val(qv)
                        return i2
                    op('qgp', mg, reads=rd, writes=['accm'], tag="mrg", nd=2)

            if upto >= 2.6:
                if upto >= 3:
                    slist = [0, 1, 2]
                elif upto >= 2.95:
                    slist = [0, 2]
                elif upto >= 2.9:
                    slist = [0, 1]
                else:
                    slist = [0]
                nh1 = NH if upto >= 2.8 else 1
                for s in slist:
                    for h in range(NH if s > 0 else nh1):
                        emit_slot(s, h)
                    if upto >= 2.8:
                        emit_merge(s)
        if upto >= 4:
            streams['act'].append(lambda e: e.wait_ge(s_cc, 1))
            for j in range(NDEC):
                def lcp(e, j=j):
                    rv = load_idx(e, mybir.EngineType.Activation, 9 + 2 * j,
                                  3 * 768)
                    sb_ = load_idx(e, mybir.EngineType.Activation, 10 + 2 * j, CH)
                    inst = e.dma_start(
                        out=latslot.ap()[j * 768:(j + 1) * 768, :],
                        in_=lat_gath.ap()[bass.ds(rv, 768), bass.ds(sb_, CH)])
                    free_val(rv, sb_)
                    return inst
                op('qsc', lcp, reads=[], writes=[f'lsl{j}'], tag="latcp")
            emit_dec(2)
        if upto >= 5:
            for s in range(3, NSLOT):
                for h in range(NH):
                    emit_slot(s, h)
                emit_merge(s)
                if s < NSLOT - 1:
                    emit_dec(s)
            for h in range(NH):
                emit_fin(h)

            if debug:
                dma('qsy', dbg["d_kst"], kst.ap(),
                    reads=[f'dec{d}kst{h}' for d in range(8) for h in range(16)],
                    writes=['dbgk'])
                dma('qsy', dbg["d_vst"], vst.ap(),
                    reads=[f'dec{d}vst{h}' for d in range(8) for h in range(16)],
                    writes=['dbgv'])
                dma('qsy', dbg["d_acc"], accd.ap(),
                    reads=['accm'], writes=['dbga'])
                dma('qsy', dbg["d_ao"], arB[:, 0:16 * CH],
                    reads=[f'ao_{q}_{c2}_{par}' for q in range(2)
                           for c2 in range(8) for par in range(2)],
                    writes=['dbgao'])

        if upto >= 6:
            dma('qgp', arA[:, 0:8 * CH].rearrange("p (i c) -> p i c", i=8),
                wo_d.rearrange("(i p) o -> p i o", p=128),
                writes=['Wo', 'arAk'], tag="ldwo")
            aoreads = [f'ao_{q}_{c2}_{par}' for q in range(2) for c2 in range(8)
                       for par in range(2)]
            for q in range(2):
                for oc in range(8):
                    for th in range(2):
                        pso = psS1[:, th * 512: th * 512 + 512]
                        pkey = 'S1a' if th == 0 else 'S1b'
                        for ic in range(8):
                            lh = arA[:, ic * CH + oc * 128: ic * CH + oc * 128 + 128]
                            op('pe', lambda e, lh=lh, pso=pso, q=q, ic=ic, th=th:
                               e.matmul(
                                   pso, lh,
                                   arB[:, (q * 8 + ic) * CH + th * 512:
                                       (q * 8 + ic) * CH + th * 512 + 512],
                                   start=(ic == 0), stop=(ic == 7),
                                   skip_group_check=True),
                               reads=['Wo'] + aoreads, writes=[pkey],
                               inc=(ic == 7), tag="womm")
                        xs = x_sb[:, oc * TOWN + q * CH + th * 512:
                                  oc * TOWN + q * CH + th * 512 + 512]
                        op('dve', lambda e, pso=pso, xs=xs: e.tensor_tensor(
                            xs, xs, pso, ALU.add),
                           reads=[pkey, f'x_{oc}'], writes=[f'x_{oc}'],
                           tag="wores")
            if debug:
                dma('qsy', dbg["d_x2"].rearrange("(c p) t -> p c t", p=128),
                    x_sb[:, :].rearrange("p (c t) -> p c t", c=8),
                    reads=[f'x_{dc}' for dc in range(8)], writes=['dbg2'])

        if upto >= 7:
            emit_ffn('F2')
        dma('qsy', xout.rearrange("(c p) t -> p c t", p=128),
            x_sb[:, :].rearrange("p (c t) -> p c t", c=8),
            reads=[f'x_{dc}' for dc in range(8)], writes=['out'], tag="stout")
        FINSEM, FINV = tk.writer['out']
        streams['sync'].append(lambda e: e.wait_ge(sems[FINSEM], FINV))

        @block.tensor
        def _(eng):
            for f in streams['pe']:
                f(eng)

        @block.scalar
        def _(eng):
            for f in streams['act']:
                f(eng)

        @block.vector
        def _(eng):
            for f in streams['dve']:
                f(eng)

        @block.sync
        def _(eng):
            for f in streams['sync']:
                f(eng)

        @block.gpsimd
        def _(eng):
            for f in streams['pool']:
                f(eng)

    return nc


_NC_CACHE = {}


def _get_nc(debug=False, upto=9):
    key = (debug, upto)
    if key not in _NC_CACHE:
        _NC_CACHE[key] = build_nc(debug, upto)
    return _NC_CACHE[key]


def kernel(L0_attn_x, L0_attn_W_kv, L0_attn_g_kv, L0_attn_W_kdec,
           L0_attn_W_vdec, L0_attn_W_q, L0_attn_W_qdec, L0_attn_W_krope,
           L0_attn_W_qrope, L0_attn_W_o, L0_ffn_w_up, L0_ffn_w_down,
           _debug=False, _results_out=None, _upto=9):
    nc = _get_nc(_debug, _upto)
    xT = [np.ascontiguousarray(np.asarray(L0_attn_x[b]).T) for b in range(B)]
    shared = {
        "wkv": np.ascontiguousarray(L0_attn_W_kv, dtype=np.float32),
        "gkv": np.ascontiguousarray(L0_attn_g_kv, dtype=np.float32)[None, :],
        "wkdec": np.ascontiguousarray(L0_attn_W_kdec, dtype=np.float32),
        "wvdec": np.ascontiguousarray(L0_attn_W_vdec, dtype=np.float32),
        "wq": np.ascontiguousarray(L0_attn_W_q, dtype=np.float32),
        "wqdec": np.ascontiguousarray(L0_attn_W_qdec, dtype=np.float32),
        "wkrope": np.ascontiguousarray(L0_attn_W_krope, dtype=np.float32),
        "wqrope": np.ascontiguousarray(L0_attn_W_qrope, dtype=np.float32),
        "wo": np.ascontiguousarray(L0_attn_W_o, dtype=np.float32),
        "wupT": np.ascontiguousarray(np.asarray(L0_ffn_w_up).T),
        "wdownT": np.ascontiguousarray(np.asarray(L0_ffn_w_down).T),
    }
    in_maps = []
    for core in range(8):
        b, c = core // 4, core % 4
        xs = np.concatenate([xT[b][:, c * CH:(c + 1) * CH],
                             xT[b][:, (7 - c) * CH:(8 - c) * CH]], axis=1)
        m = dict(shared)
        m["xin"] = np.ascontiguousarray(xs)
        m["trig"] = make_trig(c)
        m["sched"] = make_sched(c)
        in_maps.append(m)
    res = run_bass_kernel_spmd(nc, in_maps, core_ids=list(range(8)))
    if _results_out is not None:
        _results_out.append(res)
    out = np.empty((B, T, D), dtype=np.float32)
    for core in range(8):
        b, c = core // 4, core % 4
        xo = res.results[core]["xout"]
        out[b, c * CH:(c + 1) * CH, :] = xo[:, 0:CH].T
        out[b, (7 - c) * CH:(8 - c) * CH, :] = xo[:, CH:2 * CH].T
    return out


# version 07
# speedup vs baseline: 1.0187x; 1.0187x over previous
import math
import numpy as np
import ml_dtypes
import concourse.bass as bass
import concourse.mybir as mybir
from concourse.bass_utils import run_bass_kernel_spmd

F32 = mybir.dt.float32
BF16 = mybir.dt.bfloat16
I32 = mybir.dt.int32
AF = mybir.ActivationFunctionType
ALU = mybir.AluOpType
AX = mybir.AxisListType.X
SP = mybir.EngineType.SP
POOL = mybir.EngineType.Pool

B, T, D = 2, 8192, 1024
NH, HD, RD, QDIM = 16, 64, 32, 96
KVR, QR, DFF = 256, 512, 4096
CH = 1024
TOWN = 2 * CH
SCALE = 1.0 / math.sqrt(96.0)
ESHIFT = -25.0
MASKVAL = -30000.0
NSLOT, NDEC = 9, 6
NS = NSLOT + NDEC * 2


def _rope_tables():
    inv_freq = 1.0 / 10000.0 ** (np.arange(0, RD, 2, dtype=np.float32) / RD)
    t = np.arange(T, dtype=np.float32)
    freqs = np.outer(t, inv_freq)
    emb = np.concatenate([freqs, freqs], -1)
    return np.cos(emb).astype(np.float32), np.sin(emb).astype(np.float32)


def make_trig(c):
    cos, sin = _rope_tables()
    pos = np.concatenate([np.arange(c * CH, (c + 1) * CH),
                          np.arange((7 - c) * CH, (8 - c) * CH)])
    ct = cos[pos].T
    st = sin[pos].T.copy()
    st[:RD // 2] *= -1.0
    return np.concatenate([np.tile(ct, (4, 1)), np.tile(st, (4, 1))],
                          axis=1).astype(np.float32)


def make_sched(c):
    A, Bc = c, 7 - c
    slots = [(A, A), (Bc, Bc), (Bc, A)]
    rest = [(A, k) for k in range(A)] + [(Bc, k) for k in range(Bc) if k != A]
    assert len(rest) == 6
    slots += rest
    vals = []
    for (qc, kc) in slots:
        vals.append(0 if qc == A else 1)
    for (_, kc) in rest:
        r = kc if kc <= 3 else 7 - kc
        sub = 0 if kc <= 3 else 1
        vals += [r * 768, sub * CH]
    assert len(vals) == NS
    return np.array([vals], dtype=np.int32)


def make_trimask():
    m = np.zeros((128, 128), dtype=np.float32)
    m[np.arange(128)[:, None] > np.arange(128)[None, :]] = MASKVAL
    return m


class TK:
    NL = 8

    def __init__(self, sems):
        self.sems = sems
        self.count = {k: 0 for k in sems}
        self.step = {k: (16 if k[0] == 'q' else 1) for k in sems}
        self.uses = {'qsy': 0, 'qgp': 0, 'qsc': 0}
        self.writer = {}
        self.readers = {}
        self.pending = {}
        self.seen = {}

    def _seen(self, who):
        return self.seen.setdefault(who, {})

    def pre(self, who, reads, writes):
        need = {}

        def w_need(w):
            if w is None:
                return
            if w[0] != who_self:
                need[w[0]] = max(need.get(w[0], 0), w[1])
        who_self = who
        for r in reads:
            w = self.writer.get(r)
            if w:
                need[w[0]] = max(need.get(w[0], 0), w[1])
        for r in writes:
            w = self.writer.get(r)
            if w:
                need[w[0]] = max(need.get(w[0], 0), w[1])
            for rs, rv in self.readers.get(r, {}).items():
                need[rs] = max(need.get(rs, 0), rv)
        out = {}
        seen = self._seen(who)
        for s, v in need.items():
            if s == who == 'pe':
                continue
            if seen.get(s, 0) < v:
                out[s] = v
                seen[s] = v
        return out

    def dma_lane(self, q, units=1):
        lane = self.uses[q] % TK.NL
        self.uses[q] += 1
        name = f'{q}{lane}'
        prev = self.count[name]
        self.count[name] = prev + 16 * units
        return name, prev, self.count[name]

    def post_compute(self, dom, reads, writes, inc):
        if inc:
            self.count[dom] += 1
        cnt = self.count[dom]
        pend = self.pending.setdefault(dom, {'r': set(), 'w': set()})
        for r in writes:
            self.writer[r] = (dom, cnt)
            self.readers[r] = {}
        for r in reads:
            self.readers.setdefault(r, {})[dom] = cnt
        if inc:
            for r in pend['r']:
                if self.readers.setdefault(r, {}).get(dom, 0) < cnt:
                    self.readers[r][dom] = cnt
            for r in pend['w']:
                w = self.writer.get(r)
                if w and w[0] == dom and w[1] < cnt:
                    self.writer[r] = (dom, cnt)
            pend['r'].clear()
            pend['w'].clear()
        else:
            pend['r'].update(reads)
            pend['w'].update(writes)

    def post_dma(self, semname, cnt, reads, writes):
        for r in writes:
            self.writer[r] = (semname, cnt)
            self.readers[r] = {}
        for r in reads:
            self.readers.setdefault(r, {})[dom := semname] = cnt


def build_nc(debug=False, upto=9):
    nc = bass.Bass("TRN2", target_bir_lowering=False)

    xin = nc.dram_tensor("xin", [D, TOWN], F32, kind="ExternalInput").ap()
    trig_d = nc.dram_tensor("trig", [128, 2 * TOWN], F32, kind="ExternalInput").ap()
    sched_d = nc.dram_tensor("sched", [1, NS], I32, kind="ExternalInput").ap()
    wkv_d = nc.dram_tensor("wkv", [D, KVR], F32, kind="ExternalInput").ap()
    gkv_d = nc.dram_tensor("gkv", [1, KVR], F32, kind="ExternalInput").ap()
    wkdec_d = nc.dram_tensor("wkdec", [KVR, D], F32, kind="ExternalInput").ap()
    wvdec_d = nc.dram_tensor("wvdec", [KVR, D], F32, kind="ExternalInput").ap()
    wq_d = nc.dram_tensor("wq", [D, QR], F32, kind="ExternalInput").ap()
    wqdec_d = nc.dram_tensor("wqdec", [QR, D], F32, kind="ExternalInput").ap()
    wkrope_d = nc.dram_tensor("wkrope", [D, QR], F32, kind="ExternalInput").ap()
    wqrope_d = nc.dram_tensor("wqrope", [QR, QR], F32, kind="ExternalInput").ap()
    wo_d = nc.dram_tensor("wo", [D, D], F32, kind="ExternalInput").ap()
    wupT_d = nc.dram_tensor("wupT", [D, DFF], F32, kind="ExternalInput").ap()
    wdownT_d = nc.dram_tensor("wdownT", [DFF, D], F32, kind="ExternalInput").ap()
    xout = nc.dram_tensor("xout", [D, TOWN], F32, kind="ExternalOutput").ap()
    dbg = {}
    if debug:
        for name, shp, dt_ in [("d_x1", [D, TOWN], F32),
                               ("d_lat", [768, TOWN], BF16),
                               ("d_qst", [NH * QDIM, TOWN], BF16),
                               ("d_kst", [NH * QDIM, 8 * CH], BF16),
                               ("d_vst", [8 * NH * 128, 520], BF16),
                               ("d_acc", [2 * NH * 65, CH], F32),
                               ("d_ao", [128, 16 * CH], BF16),
                               ("d_x2", [D, TOWN], F32),
                               ("d_wup", [16 * 128, 2048], BF16),
                               ("d_wdn", [4 * 4 * 128, 2048], BF16),
                               ("d_xn", [128, 4096], BF16),
                               ("d_h", [128, 16 * CH], BF16),
                               ("d_fin", [128, CH], F32),
                               ("d_st", [1, 2048], F32),
                               ("d_pt", [128, 2 * CH], BF16)]:
            dbg[name] = nc.dram_tensor(name, shp, dt_, kind="ExternalOutput").ap()

    lat_own = nc.dram_tensor("lat_own", [768, TOWN], BF16)
    lat_gath = nc.dram_tensor("lat_gath", [4 * 768, TOWN], BF16)
    wup_t = nc.dram_tensor("wup_t", [16 * 128, 2048], BF16)
    wdn_t = nc.dram_tensor("wdn_t", [4 * 4 * 128, 2048], BF16)
    kst = nc.dram_tensor("kst", [NH * QDIM, 8 * CH], BF16)
    vst = nc.dram_tensor("vst", [8 * NH * 128, 520], BF16)
    qst = nc.dram_tensor("qst", [NH * QDIM, TOWN], BF16)
    accd = nc.dram_tensor("accd", [2 * NH * 65, CH], F32)
    accd2 = nc.dram_tensor("accd2", [NSLOT * NH * 65, CH], F32)
    latslot = nc.dram_tensor("latslot", [NDEC * 768, CH], BF16)
    qstslot = nc.dram_tensor("qstslot", [NSLOT * NH * QDIM, CH], BF16)
    rpst = nc.dram_tensor("rpst", [128, TOWN], F32)

    tri_t = nc.inline_tensor(make_trimask(), "trimask")
    onesb_t = nc.inline_tensor(np.ones((128, 1), dtype=ml_dtypes.bfloat16), "onesbt")
    onesf_t = nc.inline_tensor(np.ones((1, 128), dtype=np.float32), "onesft")

    from contextlib import ExitStack
    with ExitStack() as _es:
        x_sb = _es.enter_context(nc.sbuf_tensor("x_s", [128, 8 * TOWN], F32))
        arA = _es.enter_context(nc.sbuf_tensor("arenaA", [128, 16 * CH], BF16))
        arB = _es.enter_context(nc.sbuf_tensor("arenaB", [128, 16 * CH], BF16))
        trig_sb = _es.enter_context(nc.sbuf_tensor("trig_s", [128, 2 * TOWN], BF16))
        fscr = _es.enter_context(nc.sbuf_tensor("fscr", [128, 4096], F32))
        stF = _es.enter_context(nc.sbuf_tensor("statsF", [1, 2048], F32))
        xn_sb = _es.enter_context(nc.sbuf_tensor("xn", [128, 4096], BF16))
        wstr = _es.enter_context(nc.sbuf_tensor("wstr", [128, 4096], BF16))
        ksb = _es.enter_context(nc.sbuf_tensor("ksb", [128, 2 * CH], BF16))
        vsb = _es.enter_context(nc.sbuf_tensor("vsb", [128, 2 * 520], BF16))
        qsb = _es.enter_context(nc.sbuf_tensor("qsb", [128, 2 * CH], BF16))
        pt_sb = _es.enter_context(nc.sbuf_tensor("pt", [128, 2 * CH], BF16))
        postg = _es.enter_context(nc.sbuf_tensor("postg", [65, 2 * CH], F32))
        fin_sb = _es.enter_context(nc.sbuf_tensor("fin", [128, 2 * CH], BF16))
        lrow = _es.enter_context(nc.sbuf_tensor("lrow", [1, CH], F32))
        tri_sb = _es.enter_context(nc.sbuf_tensor("tri", [128, 128], F32))
        onesb = _es.enter_context(nc.sbuf_tensor("onesb", [128, 1], BF16))
        onesf = _es.enter_context(nc.sbuf_tensor("onesf", [1, 128], F32))
        b25 = _es.enter_context(nc.sbuf_tensor("b25", [128, 1], F32))
        onesc = _es.enter_context(nc.sbuf_tensor("onesc", [128, 1], F32))
        gkv_sb = _es.enter_context(nc.sbuf_tensor("gkv_s", [128, 2], F32))
        sched_sb = _es.enter_context(nc.sbuf_tensor("schedsb", [1, NS], I32))
        psS0 = _es.enter_context(nc.psum_tensor("psS0", [128, CH], F32))
        psS1 = _es.enter_context(nc.psum_tensor("psS1", [128, CH], F32))
        psPO = _es.enter_context(nc.psum_tensor("psPO", [65, CH], F32))
        psX = _es.enter_context(nc.psum_tensor("psX", [128, 512], F32))
        s_pe = _es.enter_context(nc.semaphore("s_pe"))
        s_act = _es.enter_context(nc.semaphore("s_act"))
        s_dve = _es.enter_context(nc.semaphore("s_dve"))
        s_cc = _es.enter_context(nc.semaphore("s_cc"))
        qlanes = {}
        for q in ('qsy', 'qgp', 'qsc'):
            for ln_ in range(8):
                qlanes[f'{q}{ln_}'] = _es.enter_context(
                    nc.semaphore(f's_{q}{ln_}'))
        block = _es.enter_context(nc.Block())

        fin32 = fin_sb.ap().bitcast(F32)

        sems = {'pe': s_pe, 'act': s_act, 'dve': s_dve}
        sems.update(qlanes)
        tk = TK(sems)
        streams = {'pe': [], 'act': [], 'dve': [], 'sync': [], 'pool': []}
        D2S = {'pe': 'pe', 'act': 'act', 'dve': 'dve', 'qsy': 'sync',
               'qgp': 'pool', 'qsc': 'act'}

        def op(dom, fn, reads=(), writes=(), inc=True, samedep=True, tag="",
               nd=1):
            is_dma = dom[0] == 'q'
            waits = tk.pre(dom, list(reads), list(writes))
            if is_dma:
                semname, prev, cnt = tk.dma_lane(dom, nd)
                if prev > 0 and tk._seen(dom).get(semname, 0) < prev:
                    waits[semname] = max(waits.get(semname, 0), prev)
                    tk._seen(dom)[semname] = prev
                tk.post_dma(semname, cnt, list(reads), list(writes))
            else:
                if dom in ('dve', 'act'):
                    inc = True
                tk.post_compute(dom, list(reads), list(writes), inc)
                semname = dom

            def run(eng, waits=waits, fn=fn, inc=inc, is_dma=is_dma,
                    semname=semname, tag=tag, nd=nd):
                for s, v in waits.items():
                    eng.wait_ge(sems[s], v)
                inst = fn(eng, sems[semname]) if nd > 1 else fn(eng)
                if is_dma:
                    inst.then_inc(sems[semname], 16)
                elif inc:
                    assert inst is not None, f"op {tag} returned None"
                    inst.then_inc(sems[semname], 1)
            streams[D2S[dom]].append(run)

        def dma(dom, dst, src, reads=(), writes=(), accum=False, samedep=True,
                slow=False, tag=""):
            kw = {}
            if accum:
                kw['accum_op'] = ALU.add
            if slow:
                kw['allow_slow_non_contiguous'] = True
            op(dom, lambda e: e.dma_start(out=dst, in_=src, **kw),
               reads, writes, samedep=samedep, tag=tag)

        def lane(i):
            return stF[0:1, i * 512:(i + 1) * 512]

        SCHED_CNT = [None]

        def load_idx(eng, eng_ty, idx, maxv):
            eng.wait_ge(sems[SCHED_CNT[0][0]], SCHED_CNT[0][1])
            return nc.values_load(
                sched_sb[0:1, idx:idx + 1], engines=[eng_ty],
                min_val=0, max_val=maxv, skip_runtime_bounds_check=True)

        FREE_STATS = {'ok': 0, 'fail': 0}

        def free_val(*vals):
            for v in vals:
                try:
                    for hreg in v.val.handles:
                        nc.free_register(hreg)
                    FREE_STATS['ok'] += 1
                except Exception as ex:
                    FREE_STATS['fail'] += 1
                    if FREE_STATS['fail'] < 3:
                        print("free_val fail:", type(ex).__name__, ex)
        nc._FREE_STATS = FREE_STATS

        dma('qgp', x_sb[:, :].rearrange("p (c t) -> p c t", c=8),
            xin.rearrange("(c p) t -> p c t", p=128),
            writes=[f'x_{dc}' for dc in range(8)], tag="ldx")
        dma('qgp', trig_sb[:, :], trig_d, writes=['trig'])
        dma('qsy', tri_sb[:, :], tri_t.ap(), writes=['tri'])
        dma('qsy', onesb[:, :], onesb_t.ap(), writes=['onesb'])
        dma('qsy', onesf[:, :], onesf_t.ap(), writes=['onesf'])
        dma('qsy', sched_sb[:, :], sched_d, writes=['sched'])
        SCHED_CNT[0] = tk.writer['sched']
        for kc in range(2):
            dma('qsy', gkv_sb[:, kc:kc + 1],
                gkv_d[0:1, kc * 128:(kc + 1) * 128].rearrange("o p -> p o"),
                writes=['gkv'], slow=True)
        op('dve', lambda e: e.memset(b25[:, :], ESHIFT), writes=['b25'])
        op('dve', lambda e: e.memset(onesc[:, :], 1.0), writes=['onesf'])

        def tern_A(wsrc, nrow, ncol, key, qload, rbase, pcol):
            nrch, hcw = nrow // 128, 1024
            nhc = ncol // hcw
            npart = nrch * nhc
            lnk = f'ln{rbase // 2}'
            lcell = lane(rbase // 2)
            chunks = []
            for rc in range(nrch):
                for hc in range(nhc):
                    i = rc * nhc + hc
                    par = i % 2
                    sl = fscr[:, (rbase + par) * hcw:(rbase + par + 1) * hcw]
                    fsk = [f'fs{2 * (rbase + par)}', f'fs{2 * (rbase + par) + 1}']
                    rw = f'{key}a{i}'
                    chunks.append((i, par, sl, fsk, rw,
                                   wsrc[rc * 128:(rc + 1) * 128,
                                        hc * hcw:(hc + 1) * hcw]))

            def emit_chunk(n):
                i, par, sl, fsk, rw, src = chunks[n]
                dma(qload, sl, src, writes=[rw] + fsk, tag=f"{key}lda")
                pr = fin32[:, pcol + i:pcol + i + 1]
                op('dve', lambda e, sl=sl, pr=pr: e.tensor_reduce(
                    pr, sl, AX, ALU.add, apply_absolute_value=True),
                    reads=[rw] + fsk,
                    writes=[f'{key}p{i}', 'finbuf'])

            def emit_alpha():
                prs = fin32[:, pcol:pcol + npart]
                op('pe', lambda e, prs=prs, npart=npart: e.matmul(
                    psX[0:1, 0:npart], onesc[:, 0:1], prs,
                    start=True, stop=True, skip_group_check=True),
                   reads=[f'{key}p{i}' for i in range(npart)] +
                         ['onesf', 'finbuf'],
                   writes=['X'], tag=f"{key}amm")
                op('dve', lambda e, npart=npart: e.tensor_reduce(
                    lcell[0:1, 0:1], psX[0:1, 0:npart], AX, ALU.add),
                   reads=['X'], writes=[f'{key}al', lnk])
                op('dve', lambda e: e.tensor_scalar(
                    lcell[0:1, 1:2], lcell[0:1, 0:1],
                    0.5 / (nrow * ncol), None, ALU.mult),
                   reads=[f'{key}al', lnk], writes=[f'{key}th', lnk])
                op('pe', lambda e: e.matmul(
                    psX[:, 0:1], onesf[0:1, :], lcell[0:1, 1:2],
                    start=True, stop=True, skip_group_check=True),
                   reads=[f'{key}th', 'onesf', lnk], writes=['X'])
                thb = fin32[:, pcol + npart:pcol + npart + 1]
                op('dve', lambda e, thb=thb: e.tensor_copy(thb, psX[:, 0:1]),
                   reads=['X'], writes=[f'{key}thb', 'finbuf'])
            return chunks, emit_chunk, emit_alpha, npart

        def tern_B(key, qload, rbase, pcol, npart, chunks, store_fn, nhc):
            thb = fin32[:, pcol + npart:pcol + npart + 1]
            hcw = 1024

            def emit_chunk(n):
                i, par, sl, fsk, _rw, src = chunks[n]
                rc, hc = i // nhc, i % nhc
                rw = f'{key}b{i}'
                dma(qload, sl, src, reads=[f'{key}thb'], writes=[rw] + fsk,
                    tag=f"{key}ldb")
                sg = arA[:, (rbase + par) * hcw:(rbase + par + 1) * hcw]
                gt = pt_sb[:, par * hcw:(par + 1) * hcw] if rbase == 0 else \
                    qsb[:, par * hcw:(par + 1) * hcw]
                gk = f'pt{par}' if rbase == 0 else f'qsb{par}'
                op('act', lambda e, sl=sl, sg=sg: e.activation(
                    sg, sl, AF.Sign),
                   reads=[rw] + fsk,
                   writes=[f'{key}sg{i}', f'sg{rbase + par}'])
                op('act', lambda e, sl=sl: e.activation(sl, sl, AF.Abs),
                   reads=[rw, f'{key}sg{i}'] + fsk, writes=[rw] + fsk)
                op('dve', lambda e, sl=sl, gt=gt, thb=thb:
                   e.tensor_scalar(gt, sl, thb, None, ALU.is_gt),
                   reads=[rw, f'{key}thb', 'finbuf'] + fsk,
                   writes=[f'{key}gt{i}', gk])
                op('dve', lambda e, sg=sg, gt=gt: e.tensor_tensor(
                    sg, sg, gt, ALU.mult),
                   reads=[f'{key}gt{i}', f'{key}sg{i}', gk],
                   writes=[f'{key}wt{i}', f'sg{rbase + par}'])
                store_fn(rc, hc, sg, [f'{key}wt{i}', f'sg{rbase + par}', gk])
            return emit_chunk

        wup_v = wup_t.ap().rearrange("(f p) c -> f p c", f=16)
        wdn_v = wdn_t.ap().rearrange("(o q p) c -> o q p c", o=4, q=4)

        def store_up(rc, hc, src, reads):
            dst = wup_v[hc * 4: hc * 4 + 4, :, rc * 256:(rc + 1) * 256]
            sl = src.rearrange("p (f c) -> p f c", f=4)
            dma('qgp', dst.transpose([1, 0, 2]), sl, reads=reads,
                writes=[f'wupt_{hc * 4 + f}_{rc}' for f in range(4)],
                tag="stup")

        def store_dn(rc, hc, src, reads):
            fcq, e8 = rc // 8, rc % 8
            dst = wdn_v[:, fcq, :, e8 * 256:(e8 + 1) * 256]
            sl = src.rearrange("p (o c) -> p o c", o=4)
            dma('qgp', dst.transpose([1, 0, 2]), sl, reads=reads,
                writes=[f'wdnt_{rc}'], tag="stdn")

        if upto >= 1:
            Uch, Ua_emit, Ua_alpha, Unp = tern_A(wupT_d, D, DFF, 'U', 'qsy',
                                                 0, 0)
            Vch, Va_emit, Va_alpha, Vnp = tern_A(wdownT_d, DFF, D, 'V', 'qsc',
                                                 2, 40)
            for n in range(32):
                Ua_emit(n)
                Va_emit(n)
            Ua_alpha()
            Va_alpha()
            Ub_emit = tern_B('U', 'qsy', 0, 0, Unp, Uch, store_up, 4)
            Vb_emit = tern_B('V', 'qsc', 2, 40, Vnp, Vch, store_dn, 1)
            for n in range(32):
                Ub_emit(n)
                Vb_emit(n)

            def emit_ln(t0, pref, with_rms):
                for dc in range(8):
                    xs = x_sb[:, dc * TOWN + t0: dc * TOWN + t0 + 512]
                    sq = xn_sb[:, dc * 512:(dc + 1) * 512]
                    op('act', lambda e, xs=xs, sq=sq: e.activation(
                        sq, xs, AF.Square),
                       reads=[f'x_{dc}'], writes=[f'{pref}x2_{dc}', 'xnbuf'])
                for dc in range(8):
                    xs = x_sb[:, dc * TOWN + t0: dc * TOWN + t0 + 512]
                    op('pe', lambda e, xs=xs, dc=dc: e.matmul(
                        psX[0:1, :], onesc[:, 0:1], xs,
                        start=(dc == 0), stop=(dc == 7), skip_group_check=True),
                       reads=[f'x_{dc}', 'onesf'], writes=['X'],
                       inc=(dc == 7), tag="snx")
                for dc in range(8):
                    sq = xn_sb[:, dc * 512:(dc + 1) * 512]
                    op('pe', lambda e, sq=sq, dc=dc: e.matmul(
                        psS0[0:1, 0:512], onesb[:, 0:1], sq,
                        start=(dc == 0), stop=(dc == 7), skip_group_check=True),
                       reads=[f'{pref}x2_{dc}', 'onesb', 'xnbuf'], writes=['S0a'],
                       inc=(dc == 7), tag="snx2")
                op('dve', lambda e: e.tensor_scalar(
                    lane(0), psX[0:1, 0:512], 1.0 / D, None, ALU.mult),
                   reads=['X'], writes=['L_m', 'ln0'])
                op('dve', lambda e: e.tensor_scalar(
                    lane(1), psS0[0:1, 0:512], 1.0 / D, None, ALU.mult),
                   reads=['S0a'], writes=['L_mx2', 'ln1'])
                op('dve', lambda e: e.tensor_tensor(
                    lane(2), lane(0), lane(0), ALU.mult),
                   reads=['L_m', 'ln0'], writes=['L_m2', 'ln2'])
                op('dve', lambda e: e.tensor_tensor(
                    lane(1), lane(1), lane(2), ALU.subtract),
                   reads=['L_mx2', 'L_m2', 'ln1', 'ln2'], writes=['L_v', 'ln1'])
                op('dve', lambda e: e.tensor_scalar(
                    lane(2), lane(1), 1e-5, None, ALU.add),
                   reads=['L_v', 'ln1'], writes=['L_u', 'ln2'])
                op('act', lambda e: e.activation(lane(3), lane(2), AF.Sqrt),
                   reads=['L_u', 'ln2'], writes=['L_su', 'ln3'])
                op('dve', lambda e: e.reciprocal(lane(2), lane(3)),
                   reads=['L_su', 'ln3'], writes=['L_rstd', 'ln2'])
                if with_rms:
                    op('dve', lambda e: e.tensor_tensor(
                        lane(3), lane(1), lane(2), ALU.mult),
                       reads=['L_v', 'L_rstd', 'ln1', 'ln2'], writes=['L_w1', 'ln3'])
                    op('dve', lambda e: e.tensor_tensor(
                        lane(3), lane(3), lane(2), ALU.mult),
                       reads=['L_w1', 'L_rstd', 'ln3', 'ln2'], writes=['L_w', 'ln3'])
                    op('act', lambda e: e.activation(lane(1), lane(3), AF.Sqrt),
                       reads=['L_w', 'ln3'], writes=['L_sw', 'ln1'])
                    op('dve', lambda e: e.tensor_scalar(
                        lane(1), lane(1), 1e-8, None, ALU.add),
                       reads=['L_sw', 'ln1'], writes=['L_den', 'ln1'])
                    op('dve', lambda e: e.reciprocal(lane(1), lane(1)),
                       reads=['L_den', 'ln1'], writes=['L_rden', 'ln1'])
                    op('dve', lambda e: e.tensor_tensor(
                        lane(3), lane(2), lane(1), ALU.mult),
                       reads=['L_rstd', 'L_rden', 'ln2', 'ln1'], writes=['L_c', 'ln3'])
                    ckey, cl = 'L_c', 3
                else:
                    ckey, cl = 'L_rstd', 2
                op('dve', lambda e, cl=cl: e.tensor_tensor(
                    lane(1), lane(0), lane(cl), ALU.mult),
                   reads=['L_m', ckey, 'ln0'], writes=['L_mc', 'ln1'])
                op('pe', lambda e, cl=cl: e.matmul(
                    psS0[:, 0:512], onesf[0:1, :], lane(cl),
                    start=True, stop=True, skip_group_check=True),
                   reads=[ckey, 'onesf'], writes=['S0a'], tag="bc_c")
                op('pe', lambda e: e.matmul(
                    psS0[:, 512:1024], onesf[0:1, :], lane(1),
                    start=True, stop=True, skip_group_check=True),
                   reads=['L_mc', 'onesf', 'ln1'], writes=['S0b'], tag="bc_mc")

            def emit_xnorm(t0, pref, dst_fn, extra_w=()):
                for dc in range(8):
                    xs = x_sb[:, dc * TOWN + t0: dc * TOWN + t0 + 512]
                    tmp = fin32[:, (dc % 2) * 512:(dc % 2) * 512 + 512]
                    xo = dst_fn(dc)
                    op('dve', lambda e, xs=xs, tmp=tmp: e.tensor_tensor(
                        tmp, xs, psS0[:, 0:512], ALU.mult),
                       reads=[f'x_{dc}', 'S0a'], writes=[f'{pref}t1_{dc}',
                                                         'finbuf'])
                    op('dve', lambda e, tmp=tmp, xo=xo: e.tensor_tensor(
                        xo, tmp, psS0[:, 512:1024], ALU.subtract),
                       reads=[f'{pref}t1_{dc}', 'S0b'],
                       writes=[f'{pref}xn_{dc}', 'xnbuf'] + list(extra_w))

            def emit_ffn(ff):
                for tt in range(4):
                    t0 = tt * 512
                    emit_ln(t0, ff, with_rms=True)
                    emit_xnorm(t0, ff, lambda dc: xn_sb[:, dc * 512:(dc + 1) * 512])
                    for fcg in range(16):
                        wpar = fcg % 2
                        wsl = wstr[:, wpar * 2048:(wpar + 1) * 2048]
                        src = wup_v[fcg]
                        wdep = [f'wupt_{fcg}_{rc}' for rc in range(8)]
                        dma('qsy', wsl, src,
                            reads=wdep, writes=[f'wup{wpar}', f'wstr{wpar}'], tag="ldwup")
                        for fcw in range(2):
                            fc = fcg * 2 + fcw
                            pso = psS1[:, (fc % 2) * 512:(fc % 2) * 512 + 512]
                            pkey = 'S1a' if fc % 2 == 0 else 'S1b'
                            for dc in range(8):
                                lh = wsl[:, dc * 256 + fcw * 128:
                                         dc * 256 + fcw * 128 + 128]
                                op('pe', lambda e, lh=lh, dc=dc, pso=pso: e.matmul(
                                    pso, lh, xn_sb[:, dc * 512:(dc + 1) * 512],
                                    start=(dc == 0), stop=(dc == 7),
                                    skip_group_check=True),
                                   reads=[f'wup{wpar}', f'wstr{wpar}', f'{ff}xn_{dc}', 'xnbuf'],
                                   writes=[pkey], inc=(dc == 7), tag="upmm")
                            hsl = arB[:, fc * 512:(fc + 1) * 512]
                            op('act', lambda e, pso=pso, hsl=hsl: e.activation(
                                hsl, pso, AF.Gelu),
                               reads=[pkey], writes=[f'{ff}h_{fc}'], tag="gelu")
                    for fc in range(32):
                        hsl = arB[:, fc * 512:(fc + 1) * 512]
                        h2 = fscr[:, 1024 + (fc % 2) * 256:
                                  1024 + (fc % 2) * 256 + 256].bitcast(BF16)
                        op('dve', lambda e, hsl=hsl, h2=h2: e.tensor_tensor(
                            h2, hsl, hsl, ALU.mult),
                           reads=[f'{ff}h_{fc}'], writes=[f'{ff}h2_{fc}', 'fs2'])
                        op('pe', lambda e, h2=h2, fc=fc: e.matmul(
                            psX[0:1, 0:512], onesb[:, 0:1], h2,
                            start=(fc == 0), stop=(fc == 31), skip_group_check=True),
                           reads=[f'{ff}h2_{fc}', 'onesb', 'fs2'], writes=['X'],
                           inc=True, tag="snh2")
                    op('dve', lambda e: e.tensor_scalar(
                        lane(0), psX[0:1, 0:512], 1.0 / DFF, None, ALU.mult),
                       reads=['X'], writes=['H_ms', 'ln0'])
                    op('act', lambda e: e.activation(lane(1), lane(0), AF.Sqrt),
                       reads=['H_ms', 'ln0'], writes=['H_rms', 'ln1'])
                    op('dve', lambda e: e.tensor_scalar(
                        lane(1), lane(1), 1e-8, None, ALU.add),
                       reads=['H_rms', 'ln1'], writes=['H_den', 'ln1'])
                    op('dve', lambda e: e.reciprocal(lane(2), lane(1)),
                       reads=['H_den', 'ln1'], writes=['H_c2', 'ln2'])
                    op('pe', lambda e: e.matmul(
                        psX[:, 0:512], onesf[0:1, :], lane(2),
                        start=True, stop=True, skip_group_check=True),
                       reads=['H_c2', 'onesf', 'ln2'], writes=['X'], tag="bc_c2")
                    c2sb = fscr[:, 512:768].bitcast(BF16)
                    op('dve', lambda e, c2sb=c2sb: e.tensor_copy(
                        c2sb, psX[:, 0:512]),
                       reads=['X'], writes=['H_c2sb', 'fs1'])
                    for fc in range(32):
                        hsl = arB[:, fc * 512:(fc + 1) * 512]
                        op('dve', lambda e, hsl=hsl, c2sb=c2sb: e.tensor_tensor(
                            hsl, hsl, c2sb, ALU.mult),
                           reads=['H_c2sb', 'fs1', f'{ff}h_{fc}'],
                           writes=[f'{ff}hn_{fc}'])
                    for ocg in range(4):
                        for fcq in range(4):
                            wpar = (ocg * 4 + fcq) % 2
                            wsl = wstr[:, wpar * 2048:(wpar + 1) * 2048]
                            src = wdn_v[ocg, fcq]
                            wdep = [f'wdnt_{fcq * 8 + e8}' for e8 in range(8)]
                            dma('qsy', wsl, src,
                                reads=wdep, writes=[f'wdn{wpar}', f'wstr{wpar}'], tag="ldwdn")
                            for ocw in range(2):
                                pso = psS1[:, ocw * 512: ocw * 512 + 512]
                                pkey = 'S1a' if ocw == 0 else 'S1b'
                                for e8 in range(8):
                                    fc = fcq * 8 + e8
                                    lh = wsl[:, e8 * 256 + ocw * 128:
                                             e8 * 256 + ocw * 128 + 128]
                                    op('pe', lambda e, lh=lh, fc=fc, pso=pso,
                                       fcq=fcq, e8=e8: e.matmul(
                                           pso, lh,
                                           arB[:, fc * 512:(fc + 1) * 512],
                                           start=(fcq == 0 and e8 == 0),
                                           stop=(fcq == 3 and e8 == 7),
                                           skip_group_check=True),
                                       reads=[f'wdn{wpar}', f'wstr{wpar}', f'{ff}hn_{fc}'],
                                       writes=[pkey],
                                       inc=(e8 == 7), tag="dnmm")
                        for ocw in range(2):
                            oc = ocg * 2 + ocw
                            pso = psS1[:, ocw * 512: ocw * 512 + 512]
                            pkey = 'S1a' if ocw == 0 else 'S1b'
                            xs = x_sb[:, oc * TOWN + t0: oc * TOWN + t0 + 512]
                            op('dve', lambda e, pso=pso, xs=xs:
                               e.scalar_tensor_tensor(
                                   xs, pso, 0.5, xs, ALU.mult, ALU.add),
                               reads=[pkey, f'x_{oc}'], writes=[f'x_{oc}'],
                               tag="resid")

            emit_ffn('F1')
            if debug:
                dma('qsy', dbg["d_wup"], wup_t.ap(),
                    reads=[f'wupt_{f}_{rc}' for f in range(16) for rc in range(8)],
                    writes=['dbgwu'])
                dma('qsy', dbg["d_wdn"], wdn_t.ap(),
                    reads=[f'wdnt_{rc}' for rc in range(32)], writes=['dbgwd'])
                dma('qsy', dbg["d_xn"], xn_sb[:, :],
                    reads=[f'F1xn_{dc}' for dc in range(8)], writes=['dbgxn'])
                dma('qsy', dbg["d_h"], arB[:, :],
                    reads=[f'F1hn_{fc}' for fc in range(32)], writes=['dbgh'])
            if debug:
                dma('qsy', dbg["d_x1"].rearrange("(c p) t -> p c t", p=128),
                    x_sb[:, :].rearrange("p (c t) -> p c t", c=8),
                    reads=[f'x_{dc}' for dc in range(8)], writes=['dbg1'])
        if upto >= 2:

            for tt in range(4):
                emit_ln(tt * 512, f'P{tt}', with_rms=False)
                emit_xnorm(tt * 512, f'P{tt}',
                           lambda dc, tt=tt: arA[:, dc * TOWN + tt * 512:
                                                 dc * TOWN + tt * 512 + 512],
                           extra_w=('sg0', 'sg1', 'sg2', 'sg3'))
            xlreads = [f'P{tt}xn_{dc}' for dc in range(8) for tt in range(4)]

            def proj(wsrc, nin, nout, rhs_arena, rhs_reads, out_fn, wkey,
                     scale=None, ps=None, pk=('S1a', 'S1b')):
                ps = psS1 if ps is None else ps
                nic, noc = nin // 128, nout // 128
                for oc in range(noc):
                    wpar = oc % 2
                    wsl = wstr[:, wpar * 2048:(wpar + 1) * 2048]
                    src = wsrc.rearrange("(i p) o -> p i o", p=128)[
                        :, :, oc * 128:(oc + 1) * 128]
                    dma('qgp', wsl[:, 0:nic * 128].rearrange(
                        "p (i c) -> p i c", i=nic), src,
                        writes=[f'{wkey}{wpar}', f'wstr{wpar}'], tag=f"ld{wkey}")
                    if scale is not None:
                        op('dve', lambda e, wsl=wsl, nic=nic: e.tensor_scalar(
                            wsl[:, 0:nic * 128], wsl[:, 0:nic * 128], scale,
                            None, ALU.mult),
                           reads=[f'{wkey}{wpar}'], writes=[f'{wkey}{wpar}'],
                           tag=f"sc{wkey}")
                    for tt in range(4):
                        pso = ps[:, (tt % 2) * 512:(tt % 2) * 512 + 512]
                        pkey = pk[0] if tt % 2 == 0 else pk[1]
                        for ic in range(nic):
                            lh = wsl[:, ic * 128:(ic + 1) * 128]
                            op('pe', lambda e, lh=lh, ic=ic, tt=tt, pso=pso,
                               nic=nic: e.matmul(
                                   pso, lh,
                                   rhs_arena[:, ic * TOWN + tt * 512:
                                             ic * TOWN + tt * 512 + 512],
                                   start=(ic == 0), stop=(ic == nic - 1),
                                   skip_group_check=True),
                               reads=[f'{wkey}{wpar}', f'wstr{wpar}'] + rhs_reads,
                               writes=[pkey], inc=(ic == nic - 1),
                               tag=f"mm{wkey}")
                        out_fn(oc, tt, pso, [pkey])

            def out_qc(oc, tt, pso, reads):
                dst = arB[:, oc * TOWN + tt * 512: oc * TOWN + tt * 512 + 512]
                op('dve', lambda e, pso=pso, dst=dst: e.tensor_copy(dst, pso),
                   reads=reads, writes=[f'qc_{oc}_{tt}'], tag="evqc")
            qcreads = [f'qc_{oc}_{tt}' for oc in range(4) for tt in range(4)]

            KVB = 12 * CH

            def out_kv(oc, tt, pso, reads):
                dst = arB[:, KVB + oc * TOWN + tt * 512:
                          KVB + oc * TOWN + tt * 512 + 512]
                op('dve', lambda e, pso=pso, dst=dst: e.tensor_copy(dst, pso),
                   reads=reads, writes=[f'kvc_{oc}_{tt}'], tag="evkv")
            proj(wkv_d, D, KVR, arA, xlreads, out_kv, 'Wkv')
            for tt in range(4):
                for oc in range(2):
                    kvs = arB[:, KVB + oc * TOWN + tt * 512:
                              KVB + oc * TOWN + tt * 512 + 512]
                    sq = fscr[:, 1024 + oc * 256:1024 + oc * 256 + 256].bitcast(BF16)
                    op('dve', lambda e, kvs=kvs, sq=sq: e.tensor_tensor(
                        sq, kvs, kvs, ALU.mult),
                       reads=[f'kvc_{oc}_{tt}'], writes=[f'kv2_{oc}', 'fs2'])
                    op('pe', lambda e, sq=sq, oc=oc: e.matmul(
                        psX[0:1, 0:512], onesb[:, 0:1], sq,
                        start=(oc == 0), stop=(oc == 1), skip_group_check=True),
                       reads=[f'kv2_{oc}', 'onesb', 'fs2'], writes=['X'],
                       inc=(oc == 1), tag="snkv")
                op('dve', lambda e: e.tensor_scalar(
                    lane(0), psX[0:1, 0:512], 1.0 / KVR, None, ALU.mult),
                   reads=['X'], writes=['K_ms', 'ln0'])
                op('dve', lambda e: e.tensor_scalar(
                    lane(0), lane(0), 1e-6, None, ALU.add),
                   reads=['K_ms', 'ln0'], writes=['K_u', 'ln0'])
                op('act', lambda e: e.activation(lane(1), lane(0), AF.Sqrt),
                   reads=['K_u', 'ln0'], writes=['K_su', 'ln1'])
                op('dve', lambda e: e.reciprocal(lane(2), lane(1)),
                   reads=['K_su', 'ln1'], writes=['K_rs', 'ln2'])
                op('pe', lambda e: e.matmul(
                    psX[:, 0:512], onesf[0:1, :], lane(2),
                    start=True, stop=True, skip_group_check=True),
                   reads=['K_rs', 'onesf', 'ln2'], writes=['X'], tag="bc_krs")
                for oc in range(2):
                    kvs = arB[:, KVB + oc * TOWN + tt * 512:
                              KVB + oc * TOWN + tt * 512 + 512]
                    op('dve', lambda e, kvs=kvs: e.tensor_tensor(
                        kvs, kvs, psX[:, 0:512], ALU.mult),
                       reads=['X', f'kvc_{oc}_{tt}'], writes=[f'kvn1_{oc}_{tt}'])
                    op('dve', lambda e, kvs=kvs, oc=oc: e.tensor_scalar(
                        kvs, kvs, gkv_sb[:, oc:oc + 1], None, ALU.mult),
                       reads=[f'kvn1_{oc}_{tt}', 'gkv'],
                       writes=[f'kvn_{oc}_{tt}'])
            kvnreads = [f'kvn_{oc}_{tt}' for oc in range(2) for tt in range(4)]
            for oc in range(2):
                dma('qsy', lat_own.ap()[oc * 128:(oc + 1) * 128, :],
                    arB[:, KVB + oc * TOWN: KVB + (oc + 1) * TOWN],
                    reads=[f'kvn_{oc}_{tt}' for tt in range(4)],
                    writes=[f'lat_kv{oc}'], tag="stkvn")

            def emit_rope(wsrc, rhs_arena, rhs_reads, nin, out_fn, wkey,
                          scale=None):
                nic = nin // 128
                for kc4 in range(4):
                    wpar = kc4 % 2
                    wsl = wstr[:, wpar * 2048:(wpar + 1) * 2048]
                    src = wsrc.rearrange("(i p) o -> p i o", p=128)[
                        :, :, kc4 * 128:(kc4 + 1) * 128]
                    dma('qgp', wsl[:, 0:nic * 128].rearrange(
                        "p (i c) -> p i c", i=nic), src,
                        writes=[f'{wkey}{wpar}', f'wstr{wpar}'], tag=f"ld{wkey}")
                    if scale is not None:
                        op('dve', lambda e, wsl=wsl, nic=nic: e.tensor_scalar(
                            wsl[:, 0:nic * 128], wsl[:, 0:nic * 128], scale,
                            None, ALU.mult),
                           reads=[f'{wkey}{wpar}'], writes=[f'{wkey}{wpar}', f'wstr{wpar}'])
                    for tt in range(4):
                        pso = psS1[:, (tt % 2) * 512:(tt % 2) * 512 + 512]
                        pkey = 'S1a' if tt % 2 == 0 else 'S1b'
                        for ic in range(nic):
                            lh = wsl[:, ic * 128:(ic + 1) * 128]
                            rhs = rhs_arena[:, ic * TOWN + tt * 512:
                                            ic * TOWN + tt * 512 + 512]
                            op('pe', lambda e, lh=lh, rhs=rhs, ic=ic, pso=pso,
                               nic=nic: e.matmul(
                                   pso, lh, rhs, start=(ic == 0),
                                   stop=(ic == nic - 1), skip_group_check=True),
                               reads=[f'{wkey}{wpar}', f'wstr{wpar}'] + rhs_reads,
                               writes=[pkey], inc=(ic == nic - 1),
                               tag=f"mm{wkey}")
                        dst = fscr[:, tt * 512:(tt + 1) * 512]
                        op('dve', lambda e, pso=pso, dst=dst: e.tensor_copy(
                            dst, pso),
                           reads=[pkey], writes=[f'{wkey}ev_{tt}',
                                                 f'fs{tt}'],
                           tag=f"ev{wkey}")
                    evr = [f'{wkey}ev_{tt}' for tt in range(4)]
                    for g4 in range(4):
                        for hh in range(2):
                            dma('qsy',
                                fscr[g4 * 32 + hh * 16: g4 * 32 + hh * 16 + 16,
                                     TOWN:2 * TOWN],
                                fscr[g4 * 32 + (1 - hh) * 16:
                                     g4 * 32 + (1 - hh) * 16 + 16, 0:TOWN],
                                reads=evr + ['fs0', 'fs1', 'fs2', 'fs3'],
                                writes=[f'{wkey}rot{g4}_{hh}', 'fs4', 'fs5',
                                        'fs6', 'fs7'],
                                tag="ldrot")
                    rotr = [f'{wkey}rot{g4}_{hh}' for g4 in range(4)
                            for hh in range(2)]
                    a = fscr[:, 0:TOWN]
                    bb = fscr[:, TOWN:2 * TOWN]
                    op('dve', lambda e, bb=bb: e.tensor_tensor(
                        bb, bb, trig_sb[:, TOWN:2 * TOWN], ALU.mult),
                       reads=rotr + ['trig'],
                       writes=[f'{wkey}r1', 'fs4', 'fs5', 'fs6', 'fs7'])
                    op('dve', lambda e, a=a: e.tensor_tensor(
                        a, a, trig_sb[:, 0:TOWN], ALU.mult),
                       reads=evr + ['trig', f'{wkey}rp'],
                       writes=[f'{wkey}r2', 'fs0', 'fs1', 'fs2', 'fs3'])
                    op('dve', lambda e, a=a, bb=bb: e.tensor_tensor(
                        a, a, bb, ALU.add),
                       reads=[f'{wkey}r1', f'{wkey}r2'],
                       writes=[f'{wkey}rope_{kc4}', 'fs0', 'fs1', 'fs2', 'fs3'])
                    out_fn(kc4, a, [f'{wkey}rope_{kc4}', 'fs0', 'fs1', 'fs2',
                                'fs3'])

            def out_krope(kc4, src, reads):
                dma('qgp', lat_own.ap()[256 + kc4 * 128: 256 + (kc4 + 1) * 128, :],
                    src, reads=reads, writes=[f'lat_kr{kc4}'], tag="stkr")
            emit_rope(wkrope_d, arA, xlreads, D, out_krope, 'Wkr')

            gwaits = tk.pre('qgp', ['lat_kv0', 'lat_kv1'] +
                            [f'lat_kr{k}' for k in range(4)], [])

            def _gather(e, gwaits=gwaits):
                for s_, v_ in gwaits.items():
                    e.wait_ge(sems[s_], v_)
                e.collective_compute(
                    "AllGather", ALU.bypass,
                    replica_groups=[[0, 1, 2, 3], [4, 5, 6, 7]],
                    ins=[lat_own.ap().opt()],
                    outs=[lat_gath.ap().opt()]).then_inc(s_cc)
            streams['pool'].append(_gather)

            proj(wq_d, D, QR, arA, xlreads, out_qc, 'Wq',
                 ps=psS0, pk=('S0a', 'S0b'))

            def qrow_content(h):
                return h * 96

            def qrow_rope(h):
                return h * 96 + 64

            def out_qrope(kc4, src, reads):
                for j in range(4):
                    h = kc4 * 4 + j
                    dma('qgp', qst.ap()[qrow_rope(h): qrow_rope(h) + 32, :],
                        src[j * 32:(j + 1) * 32, :], reads=reads,
                        writes=[f'qst_r{h}'], tag="stqr")
            emit_rope(wqrope_d, arB, qcreads, QR, out_qrope, 'Wqr', scale=SCALE)

            def out_qdec(oc, tt, pso, reads):
                dst = xn_sb[:, tt * 512:(tt + 1) * 512]
                op('dve', lambda e, pso=pso, dst=dst: e.tensor_copy(dst, pso),
                   reads=reads, writes=[f'qd_{oc}_{tt}', 'xnbuf'], tag="evqd")
                if tt == 3:
                    for hh in range(2):
                        h = oc * 2 + hh
                        dma('qgp', qst.ap()[qrow_content(h): qrow_content(h) + 64, :],
                            xn_sb[hh * 64:(hh + 1) * 64, 0:TOWN],
                            reads=[f'qd_{oc}_{t}' for t in range(4)] + ['xnbuf'],
                            writes=[f'qst_c{h}'], tag="stqd")
            proj(wqdec_d, QR, D, arB, qcreads, out_qdec, 'Wqd', scale=SCALE,
                 ps=psS0, pk=('S0a', 'S0b'))
            qstreads = [f'qst_c{h}' for h in range(16)] + \
                       [f'qst_r{h}' for h in range(16)]

            if debug:
                dma('qsy', dbg["d_lat"], lat_own.ap(),
                    reads=['lat_kv0', 'lat_kv1'] + [f'lat_kr{k}' for k in range(4)],
                    writes=['dbgl'])
                dma('qsy', dbg["d_qst"], qst.ap(), reads=qstreads, writes=['dbgq'])

            dma('qgp', xn_sb[:, 0:2048].rearrange("p (i c) -> p i c", i=2),
                wkdec_d.rearrange("(i p) o -> p i o", p=128),
                writes=['Wkd', 'xnbuf'] + [f'qd_{oc}_{tt}' for oc in range(8)
                                           for tt in range(4)], tag="ldwkd")
            dma('qgp', xn_sb[:, 2048:4096].rearrange("p (i c) -> p i c", i=2),
                wvdec_d.rearrange("(i p) o -> p i o", p=128),
                writes=['Wvd', 'xnbuf'], tag="ldwvd")
            op('dve', lambda e: e.memset(
                arB[:, 0:8320].rearrange("p (h s c) -> p h s c", h=16, s=8)
                [:, :, :, 64:65], 1.0),
               reads=[f'Wqr{i}' for i in range(2)],
               writes=['vones', 'arBv'] + qcreads, tag="vones")

            def emit_dec(di):
                key = f'dec{di}'
                own = di < 2
                if own:
                    kv_rhs = [arB[:, KVB + oc * TOWN + di * CH:
                                  KVB + oc * TOWN + di * CH + CH]
                              for oc in range(2)]
                    kvreads = list(kvnreads)
                else:
                    j = di - 2
                    kvl = fscr[:, 0:1024].bitcast(BF16)
                    for i2 in range(2):
                        dma('qsy', kvl[:, i2 * CH:(i2 + 1) * CH],
                            latslot.ap()[j * 768 + i2 * 128:
                                         j * 768 + (i2 + 1) * 128, :],
                            reads=[f'lsl{j}'],
                            writes=[f'{key}kv', 'fs0', 'fs1'], tag="ldkvg")
                    kv_rhs = [kvl[:, 0:CH], kvl[:, CH:2 * CH]]
                    kvreads = [f'{key}kv', 'fs0', 'fs1']
                for h in range(NH):
                    oc, hh = h // 2, h % 2
                    for th in range(2):
                        pp = (h * 2 + th) % 2
                        pso = psS1[0:64, pp * 512: pp * 512 + 512]
                        pkey = 'S1a' if pp == 0 else 'S1b'
                        for ic in range(2):
                            lh = xn_sb[:, ic * 1024 + oc * 128 + hh * 64:
                                       ic * 1024 + oc * 128 + hh * 64 + 64]
                            op('pe', lambda e, lh=lh, pso=pso, ic=ic,
                               rhs=kv_rhs[ic], th=th: e.matmul(
                                   pso, lh, rhs[:, th * 512:(th + 1) * 512],
                                   start=(ic == 0), stop=(ic == 1),
                                   skip_group_check=True),
                               reads=['Wkd', 'xnbuf'] + kvreads, writes=[pkey],
                               inc=(ic == 1), tag="kdmm")
                        dst = arA[0:64, h * CH + th * 512: h * CH + th * 512 + 512]
                        xfence = xlreads if own else []
                        if h % 2 == 0:
                            op('dve', lambda e, pso=pso, dst=dst:
                               e.tensor_copy(dst, pso),
                               reads=[pkey], writes=[f'{key}kc_{h}_{th}',
                                                     'arAk'] + xfence, tag="kdev")
                        else:
                            op('act', lambda e, pso=pso, dst=dst:
                               e.copy(dst, pso),
                               reads=[pkey], writes=[f'{key}kc_{h}_{th}',
                                                     'arAk'] + xfence, tag="kdev")
                for h in range(NH):
                    dst = arA[64:96, h * CH:(h + 1) * CH]
                    if own:
                        src_ = lat_own.ap()[256 + h * 32: 256 + (h + 1) * 32,
                                            di * CH:(di + 1) * CH]
                        dma('qsy', dst, src_,
                            reads=[f'lat_kr{h // 4}'],
                            writes=[f'{key}kr_{h}', 'arAk'] + xlreads,
                            tag="ldkrow")
                    else:
                        dma('qsy', dst,
                            latslot.ap()[j * 768 + 256 + h * 32:
                                         j * 768 + 256 + (h + 1) * 32, :],
                            reads=[f'lsl{j}'],
                            writes=[f'{key}kr_{h}', 'arAk'], tag="ldkrg")
                kreads = [f'{key}kc_{h}_{th}' for h in range(16)
                          for th in range(2)] + \
                         [f'{key}kr_{h}' for h in range(16)]
                for h in range(NH):
                    dma('qsc', kst.ap()[h * 96:(h + 1) * 96,
                                        di * CH:(di + 1) * CH],
                        arA[0:96, h * CH:(h + 1) * CH],
                        reads=kreads + ['arAk'], writes=[f'{key}kst{h}'],
                        tag="stkst")
                for ts in range(8):
                    for dh in range(2):
                        pp = (ts * 2 + dh) % 2
                        pso = psS1[:, pp * 512: pp * 512 + 512]
                        pkey = 'S1a' if pp == 0 else 'S1b'
                        for ic in range(2):
                            lh = kv_rhs[ic][:, ts * 128:(ts + 1) * 128]
                            op('pe', lambda e, lh=lh, pso=pso, ic=ic, dh=dh:
                               e.matmul(
                                   pso, lh,
                                   xn_sb[:, 2048 + ic * 1024 + dh * 512:
                                         2048 + ic * 1024 + dh * 512 + 512],
                                   start=(ic == 0), stop=(ic == 1),
                                   skip_group_check=True),
                               reads=['Wvd', 'xnbuf'] + kvreads, writes=[pkey],
                               inc=(ic == 1), tag="vdmm")
                        dst = arB[:, 0:8320].rearrange(
                            "p (h s c) -> p h s c", h=16, s=8)[
                            :, dh * 8:(dh + 1) * 8, ts, 0:64]
                        op('dve', lambda e, pso=pso, dst=dst: e.tensor_copy(
                            dst, pso.rearrange("p (h c) -> p h c", h=8)),
                           reads=[pkey], writes=[f'{key}v_{ts}_{dh}', 'arBv'],
                           tag="vdev")
                vreads = [f'{key}v_{ts}_{dh}' for ts in range(8)
                          for dh in range(2)]
                for h in range(NH):
                    dma('qsc', vst.ap()[di * 2048 + h * 128:
                                        di * 2048 + (h + 1) * 128, :],
                        arB[:, h * 520:(h + 1) * 520],
                        reads=vreads + ['vones', 'arBv'],
                        writes=[f'{key}vst{h}'], tag="stvst")

            acc_v = accd.ap().rearrange("(q x) t -> q x t", q=2)

            def kst_reads(s):
                if s <= 2:
                    return [f'dec0kst{h}' for h in range(16)] + \
                           [f'dec1kst{h}' for h in range(16)]
                return [f'dec{2 + j}kst{h}' for j in range(s - 2) for h in range(16)]

            def vst_reads(s):
                if s <= 2:
                    return [f'dec0vst{h}' for h in range(16)] + \
                           [f'dec1vst{h}' for h in range(16)]
                return [f'dec{2 + j}vst{h}' for j in range(s - 2) for h in range(16)]

            def emit_slot(s, h):
                idx = s * 16 + h
                spar = idx % 2
                kb = 0
                diag = s < 2
                key = f'sl{s}_{h}'
                m = s if s < 2 else (0 if s == 2 else s - 1)

                dma('qsy', ksb[kb:kb + 96, spar * CH:(spar + 1) * CH],
                    kst.ap()[h * 96:(h + 1) * 96, m * CH:(m + 1) * CH],
                    reads=[f'dec{m}kst{h}'], writes=[f'ksb{spar}'], tag="ldK")
                dma('qsy', vsb[:, spar * 520:(spar + 1) * 520],
                    vst.ap()[m * 2048 + h * 128: m * 2048 + (h + 1) * 128, :],
                    reads=[f'dec{m}vst{h}'], writes=[f'vsb{spar}'], tag="ldV")
                dma('qsy', qsb[kb:kb + 96, spar * CH:(spar + 1) * CH],
                    qstslot.ap()[s * 1536 + h * 96: s * 1536 + (h + 1) * 96, :],
                    reads=[f'qsl{s}'], writes=[f'qsb{spar}'], tag="ldQ")

                def emit_pv(j, segs):
                    jpar = j % 2
                    lhv = vsb[:, spar * 520 + j * 65: spar * 520 + j * 65 + 65]
                    for si, (off, sw) in enumerate(segs):
                        op('pe', lambda e, off=off, sw=sw, lhv=lhv, jpar=jpar,
                           j=j: e.matmul(
                               psPO[:, off:off + sw], lhv,
                               pt_sb[:, jpar * CH + off: jpar * CH + off + sw],
                               start=(j == 0), stop=(j == 7),
                               skip_group_check=True),
                           reads=[f'vsb{spar}', f'pt{jpar}'], writes=['PO'],
                           inc=(si == len(segs) - 1), tag="pv")

                prev_pv = None
                for j in range(8):
                    jpar = j % 2
                    psS = psS0 if jpar == 0 else psS1
                    sk = 'S0' if jpar == 0 else 'S1'
                    skeys = [sk + 'a', sk + 'b']
                    w0 = j * 128 if diag else 0
                    wd = CH - w0
                    segs = []
                    soff = w0
                    while soff < CH:
                        slim = min((soff // 512 + 1) * 512, CH)
                        segs.append((soff, slim - soff))
                        soff = slim
                    lhk = ksb[kb:kb + 96, spar * CH + j * 128:
                              spar * CH + j * 128 + 128]
                    for si, (off, sw) in enumerate(segs):
                        op('pe', lambda e, off=off, sw=sw, lhk=lhk, psS=psS,
                           spar=spar, kb=kb: e.matmul(
                               psS[:, off:off + sw], lhk,
                               qsb[kb:kb + 96, spar * CH + off:
                                   spar * CH + off + sw],
                               start=True, stop=True, skip_group_check=True),
                           reads=[f'ksb{spar}', f'qsb{spar}'], writes=skeys,
                           inc=(si == len(segs) - 1), tag="qk")
                    if diag:
                        op('dve', lambda e, psS=psS, w0=w0: e.tensor_tensor(
                            psS[:, w0:w0 + 128], psS[:, w0:w0 + 128],
                            tri_sb[:, :], ALU.add),
                           reads=skeys + ['tri'], writes=skeys, tag="mask")
                    op('act', lambda e, psS=psS, w0=w0, wd=wd, jpar=jpar:
                       e.activation(
                           pt_sb[:, jpar * CH + w0: jpar * CH + w0 + wd],
                           psS[:, w0:w0 + wd], AF.Exp, bias=b25[:, 0:1],
                           scale=1.0),
                       reads=skeys + ['b25'], writes=[f'pt{jpar}'], tag="exp")
                    if prev_pv is not None:
                        emit_pv(*prev_pv)
                    prev_pv = (j, segs)
                emit_pv(*prev_pv)
                ppar = idx % 2
                op('dve', lambda e, ppar=ppar: e.tensor_copy(
                    postg[:, ppar * CH:(ppar + 1) * CH], psPO[:, :]),
                   reads=['PO'], writes=[f'pg{ppar}'], tag="poev")

                dma('qgp', accd2.ap()[(s * 16 + h) * 65:(s * 16 + h) * 65 + 65, :],
                    postg[:, ppar * CH:(ppar + 1) * CH],
                    reads=[f'pg{ppar}'], writes=[f'a2_{s}_{h}'],
                    samedep=False, tag="stacc")

            def emit_fin(h):
                key = f'fin{h}'
                rb = 0 if h % 2 == 0 else 64
                for q in range(2):
                    dma('qsy', fin32[rb:rb + 64, 0:CH],
                        accd.ap()[(q * 16 + h) * 65:(q * 16 + h) * 65 + 64, :],
                        reads=['accm'], writes=[f'{key}o{q}', 'finbuf'],
                        tag="ldfin")
                    dma('qsy', lrow[0:1, :],
                        accd.ap()[(q * 16 + h) * 65 + 64:(q * 16 + h) * 65 + 65, :],
                        reads=['accm'], writes=['lrow'], tag="ldl")
                    for half in range(2):
                        op('pe', lambda e, rb=rb, half=half: e.matmul(
                            psX[rb:rb + 64, 0:512], onesf[0:1, 0:64],
                            lrow[0:1, half * 512:(half + 1) * 512],
                            start=True, stop=True, skip_group_check=True,
                            tile_position=(0, rb)),
                           reads=['lrow', 'onesf'], writes=['X'], tag="bcl")
                        op('dve', lambda e, rb=rb: e.reciprocal(
                            psX[rb:rb + 64, 0:512], psX[rb:rb + 64, 0:512]),
                           reads=['X'], writes=['X'], tag="recl")
                        dst = arB[rb:rb + 64,
                                  (q * 8 + h // 2) * CH + half * 512:
                                  (q * 8 + h // 2) * CH + half * 512 + 512]
                        op('dve', lambda e, rb=rb, half=half, dst=dst, q=q:
                           e.tensor_tensor(
                               dst, fin32[rb:rb + 64,
                                          half * 512: half * 512 + 512],
                               psX[rb:rb + 64, 0:512], ALU.mult),
                           reads=[f'{key}o{q}', 'X', 'finbuf'],
                           writes=[f'ao_{q}_{h // 2}_{h % 2}', 'arBv'] +
                                  (kvnreads if q == 1 else []),
                           tag="findiv")

        if upto >= 2.2:
            qst_q = qst.ap().rearrange("r (q t) -> r q t", q=2)
            for s in range(NSLOT):
                dstq = qstslot.ap()[s * 1536:(s + 1) * 1536, :]
                if s < 3:
                    qv_static = [0, 1, 1][s]
                    dma('qsc', dstq, qst_q[:, qv_static, :],
                        reads=qstreads, writes=[f'qsl{s}'], tag="qslcp")
                else:
                    def qcp(e, s=s, dstq=dstq):
                        qv = load_idx(e, mybir.EngineType.Activation, s, 1)
                        inst = e.dma_start(
                            out=dstq,
                            in_=qst_q[:, bass.ds(qv, 1), :].squeeze(1))
                        free_val(qv)
                        return inst
                    op('qsc', qcp, reads=qstreads, writes=[f'qsl{s}'], tag="qslcp")

        if upto >= 2.4:
            emit_dec(0)
            emit_dec(1)

            def emit_merge(s):
                blk_src = accd2.ap()[s * 1040:(s + 1) * 1040, :]
                rd = [f'a2_{s}_{h}' for h in range(NH)]
                if s < 3:
                    qv_static = [0, 1, 1][s]
                    dst = accd.ap()[qv_static * 1040:(qv_static + 1) * 1040, :]
                    if s == 2:
                        for cc in range(2):
                            dma('qgp', dst[:, cc * 512:(cc + 1) * 512],
                                blk_src[:, cc * 512:(cc + 1) * 512],
                                reads=rd, writes=['accm'],
                                accum=True, tag="mrg")
                    else:
                        dma('qgp', dst, blk_src, reads=rd, writes=['accm'],
                            tag="mrg")
                else:
                    def mg(e, sem, s=s, blk_src=blk_src):
                        qv = load_idx(e, POOL, s, 1)
                        i1 = e.dma_start(
                            out=acc_v[bass.ds(qv, 1), :, 0:512].squeeze(0),
                            in_=blk_src[:, 0:512], accum_op=ALU.add)
                        i1.then_inc(sem, 16)
                        i2 = e.dma_start(
                            out=acc_v[bass.ds(qv, 1), :, 512:1024].squeeze(0),
                            in_=blk_src[:, 512:1024], accum_op=ALU.add)
                        free_val(qv)
                        return i2
                    op('qgp', mg, reads=rd, writes=['accm'], tag="mrg", nd=2)

            if upto >= 2.6:
                if upto >= 3:
                    slist = [0, 1, 2]
                elif upto >= 2.95:
                    slist = [0, 2]
                elif upto >= 2.9:
                    slist = [0, 1]
                else:
                    slist = [0]
                nh1 = NH if upto >= 2.8 else 1
                for s in slist:
                    for h in range(NH if s > 0 else nh1):
                        emit_slot(s, h)
                    if upto >= 2.8:
                        emit_merge(s)
        if upto >= 4:
            streams['act'].append(lambda e: e.wait_ge(s_cc, 1))
            for j in range(NDEC):
                def lcp(e, j=j):
                    rv = load_idx(e, mybir.EngineType.Activation, 9 + 2 * j,
                                  3 * 768)
                    sb_ = load_idx(e, mybir.EngineType.Activation, 10 + 2 * j, CH)
                    inst = e.dma_start(
                        out=latslot.ap()[j * 768:(j + 1) * 768, :],
                        in_=lat_gath.ap()[bass.ds(rv, 768), bass.ds(sb_, CH)])
                    free_val(rv, sb_)
                    return inst
                op('qsc', lcp, reads=[], writes=[f'lsl{j}'], tag="latcp")
            emit_dec(2)
        if upto >= 5:
            for s in range(3, NSLOT):
                for h in range(NH):
                    emit_slot(s, h)
                emit_merge(s)
                if s < NSLOT - 1:
                    emit_dec(s)
            for h in range(NH):
                emit_fin(h)

            if debug:
                dma('qsy', dbg["d_kst"], kst.ap(),
                    reads=[f'dec{d}kst{h}' for d in range(8) for h in range(16)],
                    writes=['dbgk'])
                dma('qsy', dbg["d_vst"], vst.ap(),
                    reads=[f'dec{d}vst{h}' for d in range(8) for h in range(16)],
                    writes=['dbgv'])
                dma('qsy', dbg["d_acc"], accd.ap(),
                    reads=['accm'], writes=['dbga'])
                dma('qsy', dbg["d_ao"], arB[:, 0:16 * CH],
                    reads=[f'ao_{q}_{c2}_{par}' for q in range(2)
                           for c2 in range(8) for par in range(2)],
                    writes=['dbgao'])

        if upto >= 6:
            dma('qgp', arA[:, 0:8 * CH].rearrange("p (i c) -> p i c", i=8),
                wo_d.rearrange("(i p) o -> p i o", p=128),
                writes=['Wo', 'arAk'], tag="ldwo")
            aoreads = [f'ao_{q}_{c2}_{par}' for q in range(2) for c2 in range(8)
                       for par in range(2)]
            for q in range(2):
                for oc in range(8):
                    for th in range(2):
                        pso = psS1[:, th * 512: th * 512 + 512]
                        pkey = 'S1a' if th == 0 else 'S1b'
                        for ic in range(8):
                            lh = arA[:, ic * CH + oc * 128: ic * CH + oc * 128 + 128]
                            op('pe', lambda e, lh=lh, pso=pso, q=q, ic=ic, th=th:
                               e.matmul(
                                   pso, lh,
                                   arB[:, (q * 8 + ic) * CH + th * 512:
                                       (q * 8 + ic) * CH + th * 512 + 512],
                                   start=(ic == 0), stop=(ic == 7),
                                   skip_group_check=True),
                               reads=['Wo'] + aoreads, writes=[pkey],
                               inc=(ic == 7), tag="womm")
                        xs = x_sb[:, oc * TOWN + q * CH + th * 512:
                                  oc * TOWN + q * CH + th * 512 + 512]
                        op('dve', lambda e, pso=pso, xs=xs: e.tensor_tensor(
                            xs, xs, pso, ALU.add),
                           reads=[pkey, f'x_{oc}'], writes=[f'x_{oc}'],
                           tag="wores")
            if debug:
                dma('qsy', dbg["d_x2"].rearrange("(c p) t -> p c t", p=128),
                    x_sb[:, :].rearrange("p (c t) -> p c t", c=8),
                    reads=[f'x_{dc}' for dc in range(8)], writes=['dbg2'])

        if upto >= 7:
            emit_ffn('F2')
        dma('qsy', xout.rearrange("(c p) t -> p c t", p=128),
            x_sb[:, :].rearrange("p (c t) -> p c t", c=8),
            reads=[f'x_{dc}' for dc in range(8)], writes=['out'], tag="stout")
        FINSEM, FINV = tk.writer['out']
        streams['sync'].append(lambda e: e.wait_ge(sems[FINSEM], FINV))

        @block.tensor
        def _(eng):
            for f in streams['pe']:
                f(eng)

        @block.scalar
        def _(eng):
            for f in streams['act']:
                f(eng)

        @block.vector
        def _(eng):
            for f in streams['dve']:
                f(eng)

        @block.sync
        def _(eng):
            for f in streams['sync']:
                f(eng)

        @block.gpsimd
        def _(eng):
            for f in streams['pool']:
                f(eng)

    return nc


_NC_CACHE = {}


def _get_nc(debug=False, upto=9):
    key = (debug, upto)
    if key not in _NC_CACHE:
        _NC_CACHE[key] = build_nc(debug, upto)
    return _NC_CACHE[key]


def kernel(L0_attn_x, L0_attn_W_kv, L0_attn_g_kv, L0_attn_W_kdec,
           L0_attn_W_vdec, L0_attn_W_q, L0_attn_W_qdec, L0_attn_W_krope,
           L0_attn_W_qrope, L0_attn_W_o, L0_ffn_w_up, L0_ffn_w_down,
           _debug=False, _results_out=None, _upto=9):
    nc = _get_nc(_debug, _upto)
    xT = [np.ascontiguousarray(np.asarray(L0_attn_x[b]).T) for b in range(B)]
    shared = {
        "wkv": np.ascontiguousarray(L0_attn_W_kv, dtype=np.float32),
        "gkv": np.ascontiguousarray(L0_attn_g_kv, dtype=np.float32)[None, :],
        "wkdec": np.ascontiguousarray(L0_attn_W_kdec, dtype=np.float32),
        "wvdec": np.ascontiguousarray(L0_attn_W_vdec, dtype=np.float32),
        "wq": np.ascontiguousarray(L0_attn_W_q, dtype=np.float32),
        "wqdec": np.ascontiguousarray(L0_attn_W_qdec, dtype=np.float32),
        "wkrope": np.ascontiguousarray(L0_attn_W_krope, dtype=np.float32),
        "wqrope": np.ascontiguousarray(L0_attn_W_qrope, dtype=np.float32),
        "wo": np.ascontiguousarray(L0_attn_W_o, dtype=np.float32),
        "wupT": np.ascontiguousarray(np.asarray(L0_ffn_w_up).T),
        "wdownT": np.ascontiguousarray(np.asarray(L0_ffn_w_down).T),
    }
    in_maps = []
    for core in range(8):
        b, c = core // 4, core % 4
        xs = np.concatenate([xT[b][:, c * CH:(c + 1) * CH],
                             xT[b][:, (7 - c) * CH:(8 - c) * CH]], axis=1)
        m = dict(shared)
        m["xin"] = np.ascontiguousarray(xs)
        m["trig"] = make_trig(c)
        m["sched"] = make_sched(c)
        in_maps.append(m)
    res = run_bass_kernel_spmd(nc, in_maps, core_ids=list(range(8)))
    if _results_out is not None:
        _results_out.append(res)
    out = np.empty((B, T, D), dtype=np.float32)
    for core in range(8):
        b, c = core // 4, core % 4
        xo = res.results[core]["xout"]
        out[b, c * CH:(c + 1) * CH, :] = xo[:, 0:CH].T
        out[b, (7 - c) * CH:(8 - c) * CH, :] = xo[:, CH:2 * CH].T
    return out
